# version 02
# baseline (speedup 1.0000x reference)
import numpy as np
import ml_dtypes
from contextlib import ExitStack

import concourse.bass as bass
import concourse.mybir as mybir
from concourse.bass_utils import run_bass_kernel_spmd

F32 = mybir.dt.float32
BF16 = mybir.dt.bfloat16
AF = mybir.ActivationFunctionType
ALU = mybir.AluOpType
AXX = mybir.AxisListType.X

B, S, D = 4, 2048, 4096
DC = 32
TPC = 1024
NE = 3
NHEAD = 32
HD = 128
VIS = 614
CSHIFT = 12.0
EPS_LN = 1e-5
QSCALE = 1.0 / float(np.sqrt(HD))
CNT = [1, 1, 2, 2, 3, 3, 4, 4]
ACTF = [AF.Tanh, AF.Gelu, AF.Silu]

_DEBUG = False
_STAGES = 3


class Sched:
    ENGINES = ("tensor", "vector", "scalar", "gpsimd", "sync")

    def __init__(self, nc, ctx):
        self.nc = nc
        self.ctx = ctx
        self.ops = {e: [] for e in self.ENGINES}
        self.cnt = {}
        self.sems = {}
        self.pending = {e: [] for e in self.ENGINES}

    def sem(self, name):
        if name not in self.sems:
            self.sems[name] = self.ctx.enter_context(self.nc.semaphore(name))
            self.cnt[name] = 0
        return self.sems[name]

    def barrier(self, sems=("pe", "act", "dve")):
        waits = [(s, self.cnt.get(s, 0)) for s in sems if self.cnt.get(s, 0) > 0]
        for e in self.ENGINES:
            self.pending[e] = list(waits)

    def emit(self, eng, fn, waits=(), inc=None, amt=1):
        val = None
        if inc is not None:
            self.sem(inc)
            self.cnt[inc] += amt
            val = self.cnt[inc]
        ws = list(self.pending[eng])
        self.pending[eng] = []
        for w in waits:
            if w is None:
                continue
            s, v = w
            if v and v > 0:
                self.sem(s)
                ws.append((s, v))
        self.ops[eng].append((fn, tuple(ws), inc, amt))
        return val

    def install(self, block):
        def mk(eng_name):
            def body(eng):
                seen = {}
                for fn, waits, inc, amt in self.ops[eng_name]:
                    for (s, v) in waits:
                        if seen.get(s, 0) < v:
                            eng.wait_ge(self.sems[s], v)
                            seen[s] = v
                    inst = fn(eng)
                    if inc is not None:
                        inst.then_inc(self.sems[inc], amt)
            return body
        block.tensor(mk("tensor"))
        block.vector(mk("vector"))
        block.scalar(mk("scalar"))
        block.gpsimd(mk("gpsimd"))
        block.sync(mk("sync"))


def build_program(debug=False, stages=3):
    nc = bass.Bass("TRN2", target_bir_lowering=False,
                   disable_frame_to_traceback=not debug)

    T = {}
    def din(name, shape):
        T[name] = nc.dram_tensor(name, shape, F32, kind="ExternalInput").ap()
    din("xin", [DC, 128, TPC])
    din("w1", [NE, DC, 128, D])
    din("w2", [NE, DC, 128, D])
    din("wq", [NHEAD, 128, D])
    din("wk", [128, D])
    din("wv", [128, D])
    din("wo", [DC, 128, D])
    din("gw", [128, DC * NE])
    din("b1", [128, NE * DC])
    din("b2", [NE, D])
    din("smalls", [1, 16])
    din("vmask", [1, TPC])
    din("zmask", [2, 128, 512])
    din("ptmask", [2, 128, 128])
    din("onesf", [1, 128])
    din("onesrow", [1, 512])
    din("ones2d", [128, 128])
    din("identf", [128, 128])

    out = nc.dram_tensor("out", [DC, 128, TPC], F32, kind="ExternalOutput").ap()

    xdram = nc.dram_tensor("xdram", [DC, 128, TPC], F32).ap()
    pay_in = nc.dram_tensor("pay_in", [1, 4104], F32).ap()
    pay_out = nc.dram_tensor("pay_out", [2, 4104], F32).ap()
    kag_in = nc.dram_tensor("kag_in", [128, TPC], BF16).ap()
    kag_out = nc.dram_tensor("kag_out", [256, TPC], BF16).ap()
    vag_in = nc.dram_tensor("vag_in", [TPC, 128], BF16).ap()
    vag_out = nc.dram_tensor("vag_out", [2 * TPC, 128], BF16).ap()
    GROUPS = [[0, 1], [2, 3], [4, 5], [6, 7]]

    with ExitStack() as ctx:
        E = ctx.enter_context
        h_sb = E(nc.sbuf_tensor("h_sb", [128, DC * TPC], BF16))
        a_sb = E(nc.sbuf_tensor("a_sb", [128, DC * TPC], BF16))
        wbuf = [E(nc.sbuf_tensor(f"wbuf{i}", [128, D], BF16)) for i in range(2)]
        crep = [E(nc.sbuf_tensor(f"crep{i}", [128, TPC], BF16)) for i in range(NE)]
        crow = [E(nc.sbuf_tensor(f"crow{i}", [1, TPC], BF16)) for i in range(NE)]
        b2row = E(nc.sbuf_tensor("b2row", [1, 1024], BF16))
        xi_b = [E(nc.sbuf_tensor(f"xi{i}", [128, 512], F32)) for i in range(2)]
        xo_b = [E(nc.sbuf_tensor(f"xo{i}", [128, 512], F32)) for i in range(2)]
        tmp_b = [E(nc.sbuf_tensor(f"tmp{i}", [128, 512], F32)) for i in range(2)]
        xfull = [E(nc.sbuf_tensor(f"xfull{i}", [128, TPC], F32)) for i in range(2)]
        mrep = E(nc.sbuf_tensor("mrep", [128, TPC], F32))
        rsrep = E(nc.sbuf_tensor("rsrep", [128, TPC], F32))
        xbf = [E(nc.sbuf_tensor(f"xbf{i}", [128, TPC], BF16)) for i in range(2)]
        sqbf = [E(nc.sbuf_tensor(f"sqbf{i}", [128, TPC], BF16)) for i in range(2)]
        junk = h_sb[:, 0:512]
        klocal = a_sb[:, 0:1024]
        vloc = [a_sb[:, 1024:1152], a_sb[:, 1152:1280]]
        pt_sl = [h_sb[:, 512:1024], h_sb[:, 1024:1536]]
        ptm_sl = [h_sb[:, 1536:1664], h_sb[:, 1664:1792]]
        aot_sl = [h_sb[:, 1792:1920], h_sb[:, 1920:2048]]
        K_sb = h_sb[:, 2048:4096]
        V_sb = h_sb[:, 4096:6144]
        zp_sb = E(nc.sbuf_tensor("zp", [128, 32], F32))
        z_sb = E(nc.sbuf_tensor("z_sb", [128, 8], F32))
        r_sb = E(nc.sbuf_tensor("r_sb", [128, 8], F32))
        rT_sb = E(nc.sbuf_tensor("rT_sb", [1, TPC], F32))
        zmask_sb = [E(nc.sbuf_tensor(f"zmask_sb{i}", [128, 512], BF16)) for i in range(2)]
        ptmask_sb = [E(nc.sbuf_tensor(f"ptmask_sb{i}", [128, 128], BF16)) for i in range(2)]
        gw_bf = E(nc.sbuf_tensor("gw_bf", [128, DC * NE], BF16))
        gw_f = E(nc.sbuf_tensor("gw_f", [128, DC * NE], F32))
        b1_sb = E(nc.sbuf_tensor("b1_sb", [128, NE * DC], F32))
        h0_sb = E(nc.sbuf_tensor("h0_sb", [128, DC], F32))
        h0g_sb = E(nc.sbuf_tensor("h0g_sb", [128, DC], F32))
        stat2 = E(nc.sbuf_tensor("stat2", [1, 2], F32))
        p0s = E(nc.sbuf_tensor("p0s", [1, 2], F32))
        p1s = E(nc.sbuf_tensor("p1s", [1, 2], F32))
        tots = E(nc.sbuf_tensor("tots", [1, 2], F32))
        strg = E(nc.sbuf_tensor("strg", [1, 2], F32))
        mr1 = E(nc.sbuf_tensor("mr1", [1, 1], F32))
        vt1 = E(nc.sbuf_tensor("vt1", [1, 1], F32))
        vt2 = E(nc.sbuf_tensor("vt2", [1, 1], F32))
        row3 = [E(nc.sbuf_tensor(f"row3_{i}", [1, NE], F32)) for i in range(6)]
        gbrow_bf = E(nc.sbuf_tensor("gbrow_bf", [1, NE], BF16))
        ind_rep = [E(nc.sbuf_tensor(f"indrep{i}", [128, NE], F32)) for i in range(2)]
        g_t = E(nc.sbuf_tensor("g_t", [128, NE], F32))
        e_t = E(nc.sbuf_tensor("e_t", [128, NE], F32))
        sc_t = E(nc.sbuf_tensor("sc_t", [128, NE], F32))
        c_t = [E(nc.sbuf_tensor(f"c_t{i}", [128, NE], F32)) for i in range(3)]
        nm1 = E(nc.sbuf_tensor("nm1", [128, 1], F32))
        ssum = E(nc.sbuf_tensor("ssum", [128, 1], F32))
        rinv = E(nc.sbuf_tensor("rinv", [128, 1], F32))
        m1c = E(nc.sbuf_tensor("m1c", [128, 1], F32))
        m3c = E(nc.sbuf_tensor("m3c", [128, 1], F32))
        s2c = E(nc.sbuf_tensor("s2c", [128, 1], F32))
        m2c = E(nc.sbuf_tensor("m2c", [128, 1], F32))
        denc = E(nc.sbuf_tensor("denc", [128, 1], F32))
        den2c = E(nc.sbuf_tensor("den2c", [128, 1], F32))
        dinvc = E(nc.sbuf_tensor("dinvc", [128, 1], F32))
        dinv5 = E(nc.sbuf_tensor("dinv5", [128, 1], F32))
        tv0 = E(nc.sbuf_tensor("tv0", [128, 8], F32))
        tv1 = E(nc.sbuf_tensor("tv1", [128, 8], F32))
        g0m = E(nc.sbuf_tensor("g0m", [1, 1], F32))
        g0n = E(nc.sbuf_tensor("g0n", [1, 1], F32))
        indt = E(nc.sbuf_tensor("indt", [1, NE], F32))
        sml = E(nc.sbuf_tensor("sml", [1, 16], F32))
        vm_sb = E(nc.sbuf_tensor("vm_sb", [1, TPC], BF16))
        onesf_sb = E(nc.sbuf_tensor("onesf_sb", [1, 128], F32))
        ones_bf = E(nc.sbuf_tensor("ones_bf", [128, 128], BF16))
        onesrow_bf = E(nc.sbuf_tensor("onesrow_bf", [1, 512], BF16))
        ident_sb = E(nc.sbuf_tensor("ident_sb", [128, 128], F32))

        PS = [E(nc.psum_tensor(f"PS{i}", [128, 512], F32)) for i in range(8)]

        S_ = Sched(nc, ctx)
        em = S_.emit
        for _s in ("pe", "act", "dve", "rxs0", "rxs1", "rxl0", "rxl1"):
            S_.sem(_s)
        store_of_chunk = {}
        bfree_ffn = {}
        xo_prev_store = {0: None, 1: None}
        xi_prev_add = {0: 0, 1: 0}
        xfull_prev = {0: None, 1: None}
        tmp_prev = {0: 0, 1: 0}

        for fn in [
            lambda s: s.dma_start(out=gw_f[:, :], in_=T["gw"]),
            lambda s: s.dma_start(out=b1_sb[:, :], in_=T["b1"]),
            lambda s: s.dma_start(out=sml[:, :], in_=T["smalls"]),
            lambda s: s.dma_start(out=onesf_sb[:, :], in_=T["onesf"]),
            lambda s: s.dma_start(out=ident_sb[:, :], in_=T["identf"]),
        ]:
            em("sync", fn, inc="aux", amt=16)
        AUX_T = S_.cnt["aux"]
        for fn in [
            lambda g: g.dma_start(out=gw_bf[:, :], in_=T["gw"]),
            lambda g: g.dma_start(out=ones_bf[:, :], in_=T["ones2d"]),
            lambda g: g.dma_start(out=zmask_sb[0][:, :], in_=T["zmask"][0]),
            lambda g: g.dma_start(out=zmask_sb[1][:, :], in_=T["zmask"][1]),
            lambda g: g.dma_start(out=ptmask_sb[0][:, :], in_=T["ptmask"][0]),
            lambda g: g.dma_start(out=ptmask_sb[1][:, :], in_=T["ptmask"][1]),
            lambda g: g.dma_start(out=vm_sb[:, :], in_=T["vmask"]),
            lambda g: g.dma_start(out=onesrow_bf[:, :], in_=T["onesrow"]),
        ]:
            em("gpsimd", fn, inc="auxg", amt=16)
        AUXG_T = S_.cnt["auxg"]

        wslot_free = [0, 0]

        def emit_wload(slot, src_ap):
            v = em("gpsimd", (lambda src_ap, slot: lambda g: g.dma_start(
                out=wbuf[slot][:, :], in_=src_ap))(src_ap, slot),
                waits=[("pe", wslot_free[slot])], inc=f"wb{slot}", amt=16)
            return (f"wb{slot}", v)

        def emit_ln(src, stats, src_is_x):
            prev_sq = {}
            pe_vals = {}
            for dc in range(DC):
                slot = dc % 2
                waits = []
                if src_is_x:
                    for half in range(2):
                        w = store_of_chunk.get((dc, half))
                        if w:
                            waits.append(w)
                if dc >= 2:
                    waits.append(("act", prev_sq[dc - 2]))
                else:
                    waits.append(xfull_prev[slot])
                lv = em("sync", (lambda dc, slot: lambda s: s.dma_start(
                    out=xfull[slot][:, :], in_=src[dc]))(dc, slot),
                    waits=waits, inc=f"lnl{slot}", amt=16)
                cw = [(f"lnl{slot}", lv)]
                if dc >= 2:
                    cw.append(("pe", pe_vals[dc - 2]))
                em("scalar", (lambda slot: lambda sc: sc.copy(
                    xbf[slot][:, :], xfull[slot][:, :]))(slot),
                    waits=cw, inc="act")
                sv = em("scalar", (lambda slot: lambda sc: sc.activation(
                    sqbf[slot][:, :], xfull[slot][:, :], AF.Square))(slot),
                    inc="act")
                prev_sq[dc] = sv
                mw = [("act", sv), ("auxg", AUXG_T)]
                for b in range(4):
                    if dc == 0 and bfree_ffn.get(b):
                        mw.append(bfree_ffn[b])
                pv = None
                for half in range(2):
                    em("tensor", (lambda slot, half, dc: lambda t: t.matmul(
                        PS[half][:, :], ones_bf[:, :],
                        xbf[slot][:, half * 512: half * 512 + 512],
                        start=(dc == 0), stop=(dc == DC - 1)))(slot, half, dc),
                        waits=mw if half == 0 else ())
                    pv = em("tensor", (lambda slot, half, dc: lambda t: t.matmul(
                        PS[2 + half][:, :], ones_bf[:, :],
                        sqbf[slot][:, half * 512: half * 512 + 512],
                        start=(dc == 0), stop=(dc == DC - 1)))(slot, half, dc),
                        inc="pe" if half == 1 else None)
                pe_vals[dc] = pv
            STATS_PE = pe_vals[DC - 1]
            v0 = None
            for half in range(2):
                sl = slice(half * 512, half * 512 + 512)
                em("vector", (lambda half, sl: lambda v: v.tensor_scalar(
                    mrep[:, sl], PS[half][:, :], 1.0 / D, None, ALU.mult))(half, sl),
                    waits=[("pe", STATS_PE), ("dve", S_.cnt["dve"]),
                           ("act", S_.cnt["act"])], inc="dve")
                v0 = em("vector", (lambda half, sl: lambda v: v.tensor_scalar(
                    tmp_b[half][:, :], PS[2 + half][:, :], 1.0 / D, None,
                    ALU.mult))(half, sl), waits=[("dve", tmp_prev[half])],
                    inc="dve")
            v1 = None
            for half in range(2):
                sl = slice(half * 512, half * 512 + 512)
                v1 = em("vector", (lambda half, sl: lambda v: v.tensor_mul(
                    xo_b[half][:, :], mrep[:, sl], mrep[:, sl]))(half, sl),
                    waits=[("dve", v0)], inc="dve")
            v2 = None
            for half in range(2):
                sl = slice(half * 512, half * 512 + 512)
                v2 = em("vector", (lambda half, sl: lambda v: v.tensor_sub(
                    rsrep[:, sl], tmp_b[half][:, :], xo_b[half][:, :]))(half, sl),
                    waits=[("dve", v1)], inc="dve")
            for b in range(4):
                bfree_ffn[b] = ("dve", v2)
            a1 = None
            for half in range(2):
                sl = slice(half * 512, half * 512 + 512)
                a1 = em("scalar", (lambda half, sl: lambda sc: sc.activation(
                    tmp_b[half][:, :], rsrep[:, sl], AF.Sqrt,
                    bias=np.float32(EPS_LN), scale=np.float32(1.0)))(half, sl),
                    waits=[("dve", v2)], inc="act")
            LNFIN = None
            for half in range(2):
                sl = slice(half * 512, half * 512 + 512)
                LNFIN = em("vector", (lambda half, sl: lambda v: v.reciprocal(
                    rsrep[:, sl], tmp_b[half][:, :]))(half, sl),
                    waits=[("act", a1)], inc="dve")
                tmp_prev[half] = LNFIN

            prev_h = {}
            prev_tail = {}
            prev_abs = {}
            for dc in range(DC):
                slot = dc % 2
                waits = [("act", prev_sq[DC - 1])] if dc < 2 else \
                        [("dve", prev_h[dc - 2])]
                lv = em("sync", (lambda dc, slot: lambda s: s.dma_start(
                    out=xfull[slot][:, :], in_=src[dc]))(dc, slot),
                    waits=waits, inc=f"lnl{slot}", amt=16)
                sw = [(f"lnl{slot}", lv), ("dve", LNFIN)]
                if dc >= 1:
                    sw.append(("dve", prev_tail[dc - 1]))
                h_v = None
                for half in range(2):
                    sl = slice(half * 512, half * 512 + 512)
                    sw2 = (sw if half == 0 else [("dve", h_v)])
                    if dc == 0:
                        sw2 = sw2 + [xo_prev_store[half]]
                    sub_v = em("vector", (lambda slot, half, sl: lambda v: v.tensor_sub(
                        xo_b[half][:, :], xfull[slot][:, sl], mrep[:, sl]))(slot, half, sl),
                        waits=sw2, inc="dve")
                    xfull_prev[slot] = ("dve", sub_v)
                    xo_prev_store[half] = None
                    xi_prev_add[half] = max(xi_prev_add[half], 0)
                    h_v = em("vector", (lambda dc, half, sl: lambda v: v.tensor_mul(
                        h_sb[:, dc * TPC + half * 512: dc * TPC + half * 512 + 512],
                        xo_b[half][:, :], rsrep[:, sl]))(dc, half, sl),
                        waits=[("dve", sub_v)], inc="dve")
                prev_h[dc] = h_v
                prev_tail[dc] = h_v
                if stats:
                    h0v = em("vector", (lambda dc: lambda v: v.tensor_scalar(
                        h0_sb[:, dc:dc + 1], xo_b[0][:, 0:1], rsrep[:, 0:1],
                        None, ALU.mult))(dc),
                        waits=[("dve", h_v)], inc="dve")
                    prev_tail[dc] = h0v
                    aw = [("dve", h_v)]
                    if dc >= 2:
                        aw.append(("pe", prev_abs[dc - 2]))
                    ab = em("scalar", (lambda dc, slot: lambda sc: sc.activation(
                        sqbf[slot][:, :], h_sb[:, dc * TPC:(dc + 1) * TPC],
                        AF.Abs))(dc, slot), waits=aw, inc="act")
                    pv = None
                    for half in range(2):
                        mw2 = [("act", ab)]
                        if dc == 0:
                            for b in (4, 5):
                                if bfree_ffn.get(b):
                                    mw2.append(bfree_ffn[b])
                        pv = em("tensor", (lambda slot, half, dc: lambda t: t.matmul(
                            PS[4 + half][:, :], ones_bf[:, :],
                            sqbf[slot][:, half * 512:half * 512 + 512],
                            start=(dc == 0), stop=(dc == DC - 1)))(slot, half, dc),
                            waits=mw2 if half == 0 else (),
                            inc="pe" if half == 1 else None)
                    prev_abs[dc] = pv
            H_DONE = prev_h[DC - 1]
            if not stats:
                return H_DONE, None
            ABS_PE = prev_abs[DC - 1]
            srow = [rT_sb[:, 0:512], rT_sb[:, 512:1024]]
            sr_v = None
            for half in range(2):
                sr_v = em("vector", (lambda half: lambda v: v.tensor_mul(
                    srow[half][:, :], PS[4 + half][0:1, :],
                    vm_sb[:, half * 512: half * 512 + 512]))(half),
                    waits=[("pe", ABS_PE), ("aux", AUX_T)], inc="dve")
            q1 = em("vector", lambda v: v.reduce_sum(vt1[:, :], srow[0][:, :], AXX),
                    waits=[("dve", sr_v)], inc="dve")
            q2 = em("vector", lambda v: v.reduce_sum(vt2[:, :], srow[1][:, :], AXX),
                    inc="dve")
            q3 = em("vector", lambda v: v.tensor_add(stat2[:, 0:1], vt1[:, :], vt2[:, :]),
                    waits=[("dve", q2), ("dve", q1)], inc="dve")
            q4 = em("vector", lambda v: v.reduce_sum(vt1[:, :], PS[4][0:1, :], AXX),
                    waits=[("dve", q3)], inc="dve")
            q5 = em("vector", lambda v: v.reduce_sum(vt2[:, :], PS[5][0:1, :], AXX),
                    inc="dve")
            q6 = em("vector", lambda v: v.tensor_add(strg[:, 0:1], vt1[:, :], vt2[:, :]),
                    waits=[("dve", q5), ("dve", q4)], inc="dve")
            q7 = em("vector", lambda v: v.tensor_sub(stat2[:, 1:2], strg[:, 0:1],
                                                     stat2[:, 0:1]),
                    waits=[("dve", q6)], inc="dve")
            bfree_ffn[4] = ("dve", q5)
            bfree_ffn[5] = ("dve", q5)
            return H_DONE, q7

        def emit_gate(stat_dve, tag):
            em("sync", lambda s: s.dma_start(out=pay_in[0:1, 0:2], in_=stat2[:, :]),
               waits=[("dve", stat_dve)], inc=tag, amt=16)
            em("sync", lambda s: s.dma_start(
                out=bass.AP(pay_in.tensor, 2, [[1, 1], [32, 128], [1, 32]]),
                in_=h0_sb[:, :]), waits=[("dve", stat_dve)], inc=tag, amt=16)
            em("sync", lambda s: s.dma_start(
                out=pay_in[0:1, 4098:4104], in_=sml[:, 8:14]),
                waits=[("aux", AUX_T)], inc=tag, amt=16)
            cc = em("gpsimd", lambda g: g.collective_compute(
                "AllGather", ALU.bypass, replica_groups=GROUPS,
                ins=[pay_in[:, :]], outs=[pay_out[:, :]]),
                waits=[(tag, S_.cnt[tag])], inc="cc")
            em("sync", lambda s: s.dma_start(out=p0s[:, :], in_=pay_out[0:1, 0:2]),
               waits=[("cc", cc)], inc=tag, amt=16)
            em("sync", lambda s: s.dma_start(out=p1s[:, :], in_=pay_out[1:2, 0:2]),
               inc=tag, amt=16)
            em("sync", lambda s: s.dma_start(
                out=h0g_sb[:, :],
                in_=bass.AP(pay_out.tensor, 2, [[1, 1], [32, 128], [1, 32]])),
                inc=tag, amt=16)
            PAY_T = S_.cnt[tag]
            d1 = em("vector", lambda v: v.tensor_add(tots[:, :], p0s[:, :], p1s[:, :]),
                    waits=[(tag, PAY_T)], inc="dve")
            d2 = em("vector", lambda v: v.tensor_mul(strg[:, :], tots[:, :],
                                                     sml[:, 6:8]),
                    waits=[("dve", d1), ("aux", AUX_T)], inc="dve")
            d3 = em("vector", lambda v: v.tensor_scalar(vt1[:, :], strg[:, 0:1],
                                                        1e-8, None, ALU.add),
                    waits=[("dve", d2)], inc="dve")
            d4 = em("vector", lambda v: v.reciprocal(vt2[:, :], vt1[:, :]),
                    waits=[("dve", d3)], inc="dve")
            d5 = em("vector", lambda v: v.tensor_mul(mr1[:, :], strg[:, 1:2],
                                                     vt2[:, :]),
                    waits=[("dve", d4)], inc="dve")
            gv = None
            for kc in range(DC):
                mw = [(tag, PAY_T), ("aux", AUX_T)]
                if kc == 0 and bfree_ffn.get(6):
                    mw.append(bfree_ffn[6])
                gv = em("tensor", (lambda kc: lambda t: t.matmul(
                    PS[6][0:1, 0:NE], h0g_sb[:, kc:kc + 1],
                    gw_f[:, kc * NE:(kc + 1) * NE],
                    start=(kc == 0), stop=(kc == DC - 1)))(kc),
                    waits=mw if kc == 0 else (),
                    inc="pe" if kc == DC - 1 else None)
            e1 = em("vector", lambda v: v.tensor_scalar(row3[0][:, :], sml[:, 0:3],
                                                        mr1[:, 0:1], None, ALU.mult),
                    waits=[("dve", d5)], inc="dve")
            e2 = em("vector", lambda v: v.tensor_add(row3[1][:, :], row3[0][:, :],
                                                     sml[:, 3:6]),
                    waits=[("dve", e1)], inc="dve")
            e3 = em("vector", lambda v: v.tensor_add(row3[2][:, :], row3[1][:, :],
                                                     PS[6][0:1, 0:NE]),
                    waits=[("dve", e2), ("pe", gv)], inc="dve")
            gb = em("scalar", lambda sc: sc.copy(gbrow_bf[:, :], row3[1][:, :]),
                    waits=[("dve", e2)], inc="act")
            f1 = em("vector", lambda v: v.reduce_max(g0m[:, :], row3[2][:, :], AXX),
                    waits=[("dve", e3)], inc="dve")
            f2 = em("vector", lambda v: v.tensor_reduce(g0n[:, :], row3[2][:, :],
                                                        AXX, ALU.min),
                    waits=[("dve", e3)], inc="dve")
            f3 = em("vector", lambda v: v.tensor_scalar(row3[3][:, :], row3[2][:, :],
                                                        g0m[:, 0:1], None, ALU.is_ge),
                    waits=[("dve", f1)], inc="dve")
            f4 = em("vector", lambda v: v.tensor_scalar(row3[4][:, :], row3[2][:, :],
                                                        g0n[:, 0:1], None, ALU.is_le),
                    waits=[("dve", f2)], inc="dve")
            f5 = em("vector", lambda v: v.tensor_add(indt[:, :], row3[3][:, :],
                                                     row3[4][:, :]),
                    waits=[("dve", f4), ("dve", f3)], inc="dve")
            f6 = em("vector", lambda v: v.tensor_scalar(row3[5][:, :], indt[:, :],
                                                        -1.0, None, ALU.mult),
                    waits=[("dve", f5)], inc="dve")
            f7 = em("vector", lambda v: v.tensor_scalar(indt[:, :], row3[5][:, :],
                                                        1.0, None, ALU.add),
                    waits=[("dve", f6)], inc="dve")
            i1 = em("tensor", lambda t: t.matmul(PS[6][:, 0:NE], onesf_sb[:, :],
                                                 row3[3][:, :], start=True, stop=True),
                    waits=[("dve", f3), ("dve", e3)], inc="pe")
            i1c = em("scalar", lambda sc: sc.copy(ind_rep[0][:, :], PS[6][:, 0:NE]),
                     waits=[("pe", i1)], inc="act")
            i2 = em("tensor", lambda t: t.matmul(PS[6][:, 0:NE], onesf_sb[:, :],
                                                 indt[:, :], start=True, stop=True),
                    waits=[("dve", f7), ("act", i1c)], inc="pe")
            i2c = em("scalar", lambda sc: sc.copy(ind_rep[1][:, :], PS[6][:, 0:NE]),
                     waits=[("pe", i2)], inc="act")

            prev_ge = None
            for tt in range(8):
                mw = [("auxg", AUXG_T)]
                if prev_ge is not None:
                    mw.append(("act", prev_ge))
                elif bfree_ffn.get(7):
                    mw.append(bfree_ffn[7])
                for kc in range(DC):
                    em("tensor", (lambda tt, kc: lambda t: t.matmul(
                        PS[7][:, 0:NE],
                        h_sb[:, kc * TPC + tt * 128: kc * TPC + tt * 128 + 128],
                        gw_bf[:, kc * NE:(kc + 1) * NE],
                        start=(kc == 0), stop=False))(tt, kc),
                        waits=mw if kc == 0 else ())
                gpe = em("tensor", lambda t: t.matmul(
                    PS[7][:, 0:NE], ones_bf[0:1, :], gbrow_bf[:, :],
                    start=False, stop=True),
                    waits=[("act", gb)], inc="pe")
                ge = em("scalar", lambda sc: sc.copy(g_t[:, :], PS[7][:, 0:NE]),
                        waits=[("pe", gpe), ("dve", S_.cnt["dve"])], inc="act")
                prev_ge = ge
                s1 = em("vector", lambda v: v.reduce_max(nm1[:, :], g_t[:, :], AXX,
                                                         negate=True),
                        waits=[("act", ge)], inc="dve")
                s2 = em("scalar", lambda sc: sc.activation(e_t[:, :], g_t[:, :],
                                                           AF.Exp, bias=nm1[:, 0:1],
                                                           scale=np.float32(1.0),
                                                           accum_out=ssum[:, :]),
                        waits=[("dve", s1)], inc="act")
                s3 = em("vector", lambda v: v.reciprocal(rinv[:, :], ssum[:, :]),
                        waits=[("act", s2)], inc="dve")
                s4 = em("vector", lambda v: v.tensor_scalar(sc_t[:, :], e_t[:, :],
                                                            rinv[:, 0:1], None,
                                                            ALU.mult),
                        waits=[("dve", s3)], inc="dve")
                u1 = em("vector", lambda v: v.reduce_max(m1c[:, :], sc_t[:, :], AXX),
                        waits=[("dve", s4)], inc="dve")
                u2 = em("vector", lambda v: v.tensor_reduce(m3c[:, :], sc_t[:, :],
                                                            AXX, ALU.min),
                        waits=[("dve", s4)], inc="dve")
                u3 = em("vector", lambda v: v.reduce_sum(s2c[:, :], sc_t[:, :], AXX),
                        waits=[("dve", s4)], inc="dve")
                u4 = em("vector", lambda v: v.tensor_sub(denc[:, :], s2c[:, :],
                                                         m3c[:, :]),
                        waits=[("dve", u3), ("dve", u2)], inc="dve")
                u5 = em("vector", lambda v: v.tensor_sub(m2c[:, :], denc[:, :],
                                                         m1c[:, :]),
                        waits=[("dve", u4), ("dve", u1)], inc="dve")
                u6 = em("vector", lambda v: v.tensor_scalar(den2c[:, :], denc[:, :],
                                                            1e-8, None, ALU.add),
                        waits=[("dve", u4)], inc="dve")
                u7 = em("vector", lambda v: v.reciprocal(dinvc[:, :], den2c[:, :]),
                        waits=[("dve", u6)], inc="dve")
                u8 = em("vector", lambda v: v.tensor_scalar(dinv5[:, :], dinvc[:, :],
                                                            0.5, None, ALU.mult),
                        waits=[("dve", u7)], inc="dve")
                u9 = em("vector", (lambda tt: lambda v: v.tensor_mul(
                    tv0[:, tt:tt + 1], m1c[:, :], dinv5[:, :]))(tt),
                    waits=[("dve", u8)], inc="dve")
                u10 = em("vector", (lambda tt: lambda v: v.tensor_mul(
                    tv1[:, tt:tt + 1], m2c[:, :], dinv5[:, :]))(tt),
                    waits=[("dve", u8), ("dve", u5)], inc="dve")
                c1 = em("vector", (lambda tt: lambda v: v.tensor_scalar(
                    c_t[0][:, :], ind_rep[0][:, :], tv0[:, tt:tt + 1], None,
                    ALU.mult))(tt),
                    waits=[("dve", u9), ("act", i1c)], inc="dve")
                c2 = em("vector", (lambda tt: lambda v: v.tensor_scalar(
                    c_t[1][:, :], ind_rep[1][:, :], tv1[:, tt:tt + 1], None,
                    ALU.mult))(tt),
                    waits=[("dve", u10), ("act", i2c)], inc="dve")
                c3 = em("vector", lambda v: v.tensor_add(c_t[2][:, :], c_t[0][:, :],
                                                         c_t[1][:, :]),
                        waits=[("dve", c2), ("dve", c1)], inc="dve")
                for e in range(NE):
                    tm = em("tensor", (lambda e: lambda t: t.matmul(
                        PS[6][0:1, 0:128], c_t[2][:, e:e + 1], ident_sb[:, :],
                        start=True, stop=True))(e),
                        waits=[("dve", c3), ("act", S_.cnt["act"])], inc="pe")
                    em("scalar", (lambda tt, e: lambda sc: sc.copy(
                        crow[e][:, tt * 128:(tt + 1) * 128],
                        PS[6][0:1, 0:128]))(tt, e),
                        waits=[("pe", tm)], inc="act")
            crep_last = None
            for e in range(NE):
                for half in range(2):
                    bank = 6 + half
                    rm = em("tensor", (lambda e, half, bank: lambda t: t.matmul(
                        PS[bank][:, :], ones_bf[0:1, :],
                        crow[e][:, half * 512: half * 512 + 512],
                        start=True, stop=True))(e, half, bank),
                        waits=[("act", S_.cnt["act"])], inc="pe")
                    crep_last = em("scalar", (lambda e, half, bank: lambda sc: sc.copy(
                        crep[e][:, half * 512: half * 512 + 512],
                        PS[bank][:, :]))(e, half, bank),
                        waits=[("pe", rm)], inc="act")
            bfree_ffn[6] = ("act", crep_last)
            bfree_ffn[7] = ("act", crep_last)
            return crep_last

        def emit_ffn_call(gate_fn, h_dve, x_src, x_dst, final_dst):
            crep_act = None
            for e in range(NE):
                wl = {0: emit_wload(0, T["w1"][e, 0]), 1: emit_wload(1, T["w1"][e, 1])}
                act_done = {}
                for fc in range(DC):
                    wsem, wval = wl[fc]
                    for half in range(2):
                        bank = 2 * (fc % 2) + half
                        mmw = [(wsem, wval)]
                        if bfree_ffn.get(bank):
                            mmw.append(bfree_ffn[bank])
                        if fc == 0 and half == 0:
                            mmw += [("dve", h_dve), ("aux", AUX_T)]
                        pv = None
                        for kc in range(DC):
                            pv = em("tensor", (lambda fc, half, kc, bank: lambda t:
                                    t.matmul(PS[bank][:, :],
                                             wbuf[fc % 2][:, kc * 128:(kc + 1) * 128],
                                             h_sb[:, kc * TPC + half * 512:
                                                  kc * TPC + half * 512 + 512],
                                             start=(kc == 0), stop=(kc == DC - 1))
                                    )(fc, half, kc, bank),
                                    waits=mmw if kc == 0 else (),
                                    inc="pe" if kc == DC - 1 else None)
                        av = em("scalar", (lambda fc, half, e, bank: lambda sc:
                                sc.activation(a_sb[:, fc * TPC + half * 512:
                                                   fc * TPC + half * 512 + 512],
                                              PS[bank][:, :], ACTF[e],
                                              bias=b1_sb[:, e * DC + fc:e * DC + fc + 1],
                                              scale=np.float32(1.0)))(fc, half, e, bank),
                                waits=[("pe", pv)], inc="act")
                        bfree_ffn[bank] = ("act", av)
                        act_done[(fc, half)] = av
                    wslot_free[fc % 2] = S_.cnt["pe"]
                    if fc + 2 < DC:
                        wl[fc + 2] = emit_wload(fc % 2, T["w1"][e, fc + 2])
                ACT_H = [act_done[(DC - 1, 0)], act_done[(DC - 1, 1)]]
                if e == 0 and gate_fn is not None:
                    crep_act = gate_fn()
                elif crep_act is None:
                    crep_act = gate_fn
                wl = {0: emit_wload(0, T["w2"][e, 0]), 1: emit_wload(1, T["w2"][e, 1])}
                b2l = em("gpsimd", (lambda e: lambda g: g.dma_start(
                    out=b2row[:, :], in_=T["b2"][e:e + 1, 0:1024]))(e),
                    waits=[("pe", wslot_free[0])], inc="b2l", amt=16)
                rmw = {}
                dst = final_dst if e == NE - 1 else x_dst
                rsrc = x_src if e == 0 else x_dst
                for dcq in range(DC):
                    if dcq in (8, 16, 24):
                        b2l = em("gpsimd", (lambda e, dcq: lambda g: g.dma_start(
                            out=b2row[:, :],
                            in_=T["b2"][e:e + 1, dcq * 128:(dcq + 8) * 128]))(e, dcq),
                            waits=[("pe", S_.cnt["pe"])], inc="b2l", amt=16)
                    wsem, wval = wl[dcq]
                    for half in range(2):
                        bank = 4 + 2 * (dcq % 2) + half
                        idx = dcq * 2 + half
                        slot = idx % 2
                        xw = [store_of_chunk.get((dcq, half)),
                              ("dve", xi_prev_add[slot])]
                        lx = em("sync", (lambda dcq, half, slot, rsrc: lambda s:
                                s.dma_start(out=xi_b[slot][:, :],
                                            in_=rsrc[dcq, :, half * 512:
                                                     half * 512 + 512])
                                )(dcq, half, slot, rsrc),
                                waits=xw, inc=f"rxl{slot}", amt=16)
                        mmw = [(wsem, wval), ("act", ACT_H[half]),
                               ("b2l", b2l), ("auxg", AUXG_T)]
                        if bfree_ffn.get(bank):
                            mmw.append(bfree_ffn[bank])
                        em("tensor", (lambda e, dcq, half, bank: lambda t: t.matmul(
                            PS[bank][:, :],
                            b2row[:, (dcq % 8) * 128:(dcq % 8) * 128 + 128],
                            onesrow_bf[0:1, :],
                            start=True, stop=False))(e, dcq, half, bank),
                            waits=mmw)
                        pv = None
                        for fc in range(DC):
                            pv = em("tensor", (lambda fc, dcq, half, bank: lambda t:
                                    t.matmul(PS[bank][:, :],
                                             wbuf[dcq % 2][:, fc * 128:(fc + 1) * 128],
                                             a_sb[:, fc * TPC + half * 512:
                                                  fc * TPC + half * 512 + 512],
                                             start=False, stop=(fc == DC - 1))
                                    )(fc, dcq, half, bank),
                                    inc="pe" if fc == DC - 1 else None)
                        mw2 = [("pe", pv), ("act", crep_act),
                               ("dve", tmp_prev[half])]
                        mu = em("vector", (lambda half, bank, e: lambda v:
                                v.tensor_mul(tmp_b[half][:, :], PS[bank][:, :],
                                             crep[e][:, half * 512:half * 512 + 512])
                                )(half, bank, e), waits=mw2, inc="dve")
                        bfree_ffn[bank] = ("dve", mu)
                        vw = [("dve", mu), (f"rxl{slot}", lx),
                              xo_prev_store[slot]]
                        dv = em("vector", (lambda slot, half: lambda v: v.tensor_add(
                                xo_b[slot][:, :], xi_b[slot][:, :], tmp_b[half][:, :])
                                )(slot, half), waits=vw, inc="dve")
                        tmp_prev[half] = dv
                        xi_prev_add[slot] = dv
                        sv = em("sync", (lambda dcq, half, slot, dst: lambda s:
                                s.dma_start(out=dst[dcq, :, half * 512:half * 512 + 512],
                                            in_=xo_b[slot][:, :]))(dcq, half, slot, dst),
                                waits=[("dve", dv)], inc=f"rxs{slot}", amt=16)
                        rmw[idx] = (dv, (f"rxs{slot}", sv))
                        xo_prev_store[slot] = (f"rxs{slot}", sv)
                        store_of_chunk[(dcq, half)] = (f"rxs{slot}", sv)
                    wslot_free[dcq % 2] = S_.cnt["pe"]
                    if dcq + 2 < DC:
                        wl[dcq + 2] = emit_wload(dcq % 2, T["w2"][e, dcq + 2])

        def emit_attention(h_dve):
            S_.barrier()
            wk_l = emit_wload(0, T["wk"])
            wv_l = emit_wload(1, T["wv"])
            kev = None
            for half in range(2):
                pv = None
                for kc in range(DC):
                    mw = [wk_l, ("dve", h_dve)]
                    if kc == 0 and bfree_ffn.get(half):
                        mw.append(bfree_ffn[half])
                    pv = em("tensor", (lambda half, kc: lambda t: t.matmul(
                        PS[half][:, :], wbuf[0][:, kc * 128:(kc + 1) * 128],
                        h_sb[:, kc * TPC + half * 512: kc * TPC + half * 512 + 512],
                        start=(kc == 0), stop=(kc == DC - 1)))(half, kc),
                        waits=mw if kc == 0 else (),
                        inc="pe" if kc == DC - 1 else None)
                kev = em("scalar", (lambda half: lambda sc: sc.copy(
                    klocal[:, half * 512: half * 512 + 512], PS[half][:, :]))(half),
                    waits=[("pe", pv)], inc="act")
                bfree_ffn[half] = ("act", kev)
            em("sync", lambda s: s.dma_start(out=kag_in[:, :], in_=klocal[:, :]),
               waits=[("act", kev)], inc="kst", amt=16)
            prev_ve = None
            for g in range(8):
                pv = None
                for kc in range(DC):
                    mw = []
                    if kc == 0:
                        mw = [wv_l, ("dve", h_dve)]
                        if prev_ve is not None:
                            mw.append(("act", prev_ve))
                        elif bfree_ffn.get(2):
                            mw.append(bfree_ffn[2])
                    pv = em("tensor", (lambda g, kc: lambda t: t.matmul(
                        PS[2][:, 0:128],
                        h_sb[:, kc * TPC + g * 128: kc * TPC + g * 128 + 128],
                        wbuf[1][:, kc * 128:(kc + 1) * 128],
                        start=(kc == 0), stop=(kc == DC - 1)))(g, kc),
                        waits=mw, inc="pe" if kc == DC - 1 else None)
                sl = g % 2
                vw = [("pe", pv)]
                if g >= 2:
                    vw.append((f"vst{sl}", 16 * (g // 2)))
                ve = em("scalar", (lambda sl: lambda sc: sc.copy(
                    vloc[sl][:, :], PS[2][:, 0:128]))(sl),
                    waits=vw, inc="act")
                prev_ve = ve
                em("sync", (lambda g, sl: lambda s: s.dma_start(
                    out=vag_in[g * 128:(g + 1) * 128, :], in_=vloc[sl][:, :]))(g, sl),
                    waits=[("act", ve)], inc=f"vst{sl}", amt=16)
            bfree_ffn[2] = ("act", prev_ve)
            KV_ST = [("kst", S_.cnt["kst"]), ("vst0", S_.cnt["vst0"]),
                     ("vst1", S_.cnt["vst1"])]
            em("gpsimd", lambda g: g.collective_compute(
                "AllGather", ALU.bypass, replica_groups=GROUPS,
                ins=[kag_in[:, :]], outs=[kag_out[:, :]]),
                waits=KV_ST, inc="cc")
            cc2 = em("gpsimd", lambda g: g.collective_compute(
                "AllGather", ALU.bypass, replica_groups=GROUPS,
                ins=[vag_in[:, :]], outs=[vag_out[:, :]]),
                inc="cc")

            wl = {0: emit_wload(0, T["wq"][0]), 1: emit_wload(1, T["wq"][1])}
            qev = {}
            for hh in range(NHEAD):
                wsem, wval = wl[hh]
                for half in range(2):
                    bank = 4 + 2 * (hh % 2) + half
                    mmw = [(wsem, wval)]
                    if hh >= 2:
                        mmw.append(("act", qev[(hh - 2, half)]))
                    elif bfree_ffn.get(bank):
                        mmw.append(bfree_ffn[bank])
                    pv = None
                    for kc in range(DC):
                        pv = em("tensor", (lambda hh, half, kc, bank: lambda t:
                                t.matmul(PS[bank][:, :],
                                         wbuf[hh % 2][:, kc * 128:(kc + 1) * 128],
                                         h_sb[:, kc * TPC + half * 512:
                                              kc * TPC + half * 512 + 512],
                                         start=(kc == 0), stop=(kc == DC - 1))
                                )(hh, half, kc, bank),
                                waits=mmw if kc == 0 else (),
                                inc="pe" if kc == DC - 1 else None)
                    qw = [("pe", pv)]
                    if hh < 2:
                        qw += KV_ST
                    qv = em("scalar", (lambda hh, half, bank: lambda sc: sc.activation(
                        a_sb[:, hh * TPC + half * 512: hh * TPC + half * 512 + 512],
                        PS[bank][:, :], AF.Copy, bias=0.0,
                        scale=np.float32(QSCALE)))(hh, half, bank),
                        waits=qw, inc="act")
                    qev[(hh, half)] = qv
                wslot_free[hh % 2] = S_.cnt["pe"]
                if hh + 2 < NHEAD:
                    wl[hh + 2] = emit_wload(hh % 2, T["wq"][hh + 2])
            Q_PE = S_.cnt["pe"]
            abank_free = {4: ("act", qev[(NHEAD - 2, 0)]),
                          5: ("act", qev[(NHEAD - 2, 1)]),
                          6: ("act", qev[(NHEAD - 1, 0)]),
                          7: ("act", qev[(NHEAD - 1, 1)])}

            for g in range(16):
                rk, li = g % 2, g // 2
                mw = [("cc", cc2), ("pe", Q_PE)] if g == 0 else ()
                em("sync", (lambda g, rk, li: lambda s: s.dma_start(
                    out=K_sb[:, g * 128:(g + 1) * 128],
                    in_=kag_out[rk * 128:(rk + 1) * 128,
                                li * 128:(li + 1) * 128]))(g, rk, li),
                    waits=mw, inc="kv", amt=16)
                em("sync", (lambda g, rk, li: lambda s: s.dma_start(
                    out=V_sb[:, g * 128:(g + 1) * 128],
                    in_=vag_out[rk * TPC + li * 128: rk * TPC + li * 128 + 128, :]
                    ))(g, rk, li), inc="kv", amt=16)
            KV_RB = S_.cnt["kv"]

            bfree = {0: bfree_ffn.get(0), 1: bfree_ffn.get(1),
                     2: bfree_ffn.get(2), 3: None}
            bfree.update(abank_free)
            zred_prev = [0] * 8
            rrec_prev = [0] * 8
            rrep_prev = [0] * 8
            zexp_prev = [0] * 8
            junk_prev = 0
            aot_read = {0: 0, 1: 0}
            pt_read = {0: None, 1: None}
            ptm_read = {0: None, 1: None}
            for hh in range(NHEAD):
                rT_rc = [0] * 8
                for j in range(8):
                    nkt = CNT[j]
                    last_ev = None
                    for kt in range(nkt):
                        bank = kt % 2
                        mmw = [("kv", KV_RB), ("act", qev[(hh, j // 4)]),
                               bfree[bank]]
                        zpv = em("tensor", (lambda hh, j, kt, bank: lambda t:
                                t.matmul(PS[bank][:, :],
                                         a_sb[:, hh * TPC + j * 128:
                                              hh * TPC + j * 128 + 128],
                                         K_sb[:, kt * 512:(kt + 1) * 512],
                                         start=True, stop=True))(hh, j, kt, bank),
                                waits=mmw, inc="pe")
                        col = j * 4 + kt
                        ew = [("dve", zred_prev[j]), ("act", junk_prev),
                              ("act", zexp_prev[j])]
                        if kt == nkt - 1:
                            mv = em("vector", (lambda bank, j: lambda v: v.tensor_add(
                                tmp_b[bank][:, :], PS[bank][:, :],
                                zmask_sb[j % 2][:, :]))(bank, j),
                                waits=[("pe", zpv), ("auxg", AUXG_T),
                                       ("act", zexp_prev[j])], inc="dve")
                            ev = em("scalar", (lambda bank, col: lambda sc:
                                    sc.activation(junk[:, :], tmp_b[bank][:, :],
                                                  AF.Exp, bias=np.float32(-CSHIFT),
                                                  scale=np.float32(1.0),
                                                  accum_out=zp_sb[:, col:col + 1])
                                    )(bank, col),
                                    waits=ew + [("dve", mv)], inc="act")
                            bfree[bank] = ("dve", mv)
                        else:
                            ev = em("scalar", (lambda bank, col: lambda sc:
                                    sc.activation(junk[:, :], PS[bank][:, :],
                                                  AF.Exp, bias=np.float32(-CSHIFT),
                                                  scale=np.float32(1.0),
                                                  accum_out=zp_sb[:, col:col + 1])
                                    )(bank, col),
                                    waits=ew + [("pe", zpv)], inc="act")
                            bfree[bank] = ("act", ev)
                        junk_prev = ev
                        last_ev = ev
                    zexp_prev[j] = last_ev
                    zr = em("vector", (lambda j, nkt: lambda v: v.reduce_sum(
                        z_sb[:, j:j + 1], zp_sb[:, j * 4:j * 4 + nkt], AXX))(j, nkt),
                        waits=[("act", last_ev), ("pe", rrec_prev[j])], inc="dve")
                    rr = em("vector", (lambda j: lambda v: v.reciprocal(
                        r_sb[:, j:j + 1], z_sb[:, j:j + 1]))(j),
                        waits=[("dve", zr)], inc="dve")
                    zred_prev[j] = rr
                    tmw = [("dve", rr), ("aux", AUX_T), bfree[3]]
                    tmv = em("tensor", (lambda j: lambda t: t.matmul(
                        PS[3][0:1, 0:128], r_sb[:, j:j + 1], ident_sb[:, :],
                        start=True, stop=True))(j),
                        waits=tmw, inc="pe")
                    rrec_prev[j] = tmv
                    rc = em("scalar", (lambda j: lambda sc: sc.copy(
                        rT_sb[:, j * 128:(j + 1) * 128], PS[3][0:1, 0:128]))(j),
                        waits=[("pe", tmv), ("pe", rrep_prev[j])], inc="act")
                    rT_rc[j] = rc
                    bfree[3] = ("act", rc)
                for qg in range(2):
                    nkt = 8 if qg == 0 else 16
                    pv_last = {}
                    for kt in range(nkt):
                        slot = kt % 2
                        bank = slot
                        stw = [("act", qev[(hh, qg)]), bfree[bank], pt_read[slot]]
                        spv = em("tensor", (lambda hh, qg, kt, bank: lambda t:
                                t.matmul(PS[bank][:, :],
                                         K_sb[:, kt * 128:(kt + 1) * 128],
                                         a_sb[:, hh * TPC + qg * 512:
                                              hh * TPC + qg * 512 + 512],
                                         start=True, stop=True))(hh, qg, kt, bank),
                                waits=stw, inc="pe")
                        pe_v = em("scalar", (lambda bank, slot: lambda sc:
                                sc.activation(pt_sl[slot][:, :], PS[bank][:, :],
                                              AF.Exp, bias=np.float32(-CSHIFT),
                                              scale=np.float32(1.0)))(bank, slot),
                                waits=[pt_read[slot], ("pe", spv)], inc="act")
                        bfree[bank] = ("act", pe_v)
                        use_mask = (8 * qg <= kt < 8 * qg + 8)
                        pmv = None
                        if use_mask:
                            jj = kt // 2 - 4 * qg
                            msel = kt % 2
                            pmv = em("vector", (lambda slot, msel, jj: lambda v:
                                    v.tensor_mul(ptm_sl[slot][:, :],
                                                 pt_sl[slot][:, jj * 128: jj * 128 + 128],
                                                 ptmask_sb[msel][:, :]))(slot, msel, jj),
                                    waits=[("act", pe_v), ("auxg", AUXG_T),
                                           ptm_read[slot]], inc="dve")
                        lastpv = None
                        for jj2 in range(4):
                            jloc = 4 * qg + jj2
                            if 2 * jloc + 1 < kt:
                                continue
                            b = 4 + jj2
                            first = (kt == 0)
                            stop = (kt == 2 * jloc + 1)
                            if use_mask and jj2 == kt // 2 - 4 * qg:
                                rhs_ap = (lambda s: lambda: ptm_sl[s][:, :])(slot)
                                pw = [("dve", pmv)]
                            else:
                                rhs_ap = (lambda s, q: lambda:
                                          pt_sl[s][:, q * 128:q * 128 + 128])(slot, jj2)
                                pw = [("act", pe_v)]
                            if first:
                                pw.append(bfree[b])
                            lastpv = em("tensor", (lambda rhs_ap, b, first, stop, kt:
                                    lambda t: t.matmul(
                                        PS[b][:, 0:128],
                                        V_sb[:, kt * 128:(kt + 1) * 128],
                                        rhs_ap(), start=first, stop=stop))(
                                        rhs_ap, b, first, stop, kt),
                                    waits=pw, inc="pe")
                            if stop:
                                pv_last[jj2] = lastpv
                        pt_read[slot] = ("pe", lastpv)
                        if use_mask:
                            ptm_read[slot] = ("pe", lastpv)
                    for jj2 in range(4):
                        jloc = 4 * qg + jj2
                        b = 4 + jj2
                        sl = jj2 % 2
                        aev = em("scalar", (lambda b, sl: lambda sc: sc.copy(
                            aot_sl[sl][:, :], PS[b][:, 0:128]))(b, sl),
                            waits=[("pe", pv_last[jj2]), ("dve", aot_read[sl])],
                            inc="act")
                        bfree[b] = ("act", aev)
                        rmw_ = [("act", rT_rc[jloc]), ("aux", AUX_T), bfree[2]]
                        rm = em("tensor", (lambda jloc: lambda t: t.matmul(
                            PS[2][:, 0:128], onesf_sb[:, :],
                            rT_sb[:, jloc * 128:(jloc + 1) * 128],
                            start=True, stop=True))(jloc),
                            waits=rmw_, inc="pe")
                        rrep_prev[jloc] = rm
                        sv = em("vector", (lambda hh, jloc, sl: lambda v:
                                v.tensor_mul(a_sb[:, hh * TPC + jloc * 128:
                                                  hh * TPC + jloc * 128 + 128],
                                             aot_sl[sl][:, :], PS[2][:, 0:128]))(
                                             hh, jloc, sl),
                                waits=[("pe", rm), ("act", aev)], inc="dve")
                        aot_read[sl] = sv
                        bfree[2] = ("dve", sv)
            AODONE = S_.cnt["dve"]
            for b in range(4, 8):
                bfree_ffn[b] = bfree[b]

            S_.barrier()
            wl = {0: emit_wload(0, T["wo"][0]), 1: emit_wload(1, T["wo"][1])}
            rmw = {}
            for dcq in range(DC):
                wsem, wval = wl[dcq]
                for half in range(2):
                    bank = (dcq % 2) * 2 + half
                    idx = dcq * 2 + half
                    slot = idx % 2
                    xw = [store_of_chunk.get((dcq, half)),
                          ("dve", xi_prev_add[slot])]
                    lx = em("sync", (lambda dcq, half, slot: lambda s: s.dma_start(
                        out=xi_b[slot][:, :],
                        in_=xdram[dcq, :, half * 512: half * 512 + 512]))(dcq, half, slot),
                        waits=xw, inc=f"rxl{slot}", amt=16)
                    mmw = [(wsem, wval), ("dve", AODONE)]
                    if idx >= 2:
                        mmw.append(("dve", rmw[idx - 2][0]))
                    pv = None
                    for hh in range(NHEAD):
                        pv = em("tensor", (lambda dcq, half, hh, bank: lambda t:
                                t.matmul(PS[bank][:, :],
                                         wbuf[dcq % 2][:, hh * 128:(hh + 1) * 128],
                                         a_sb[:, hh * TPC + half * 512:
                                              hh * TPC + half * 512 + 512],
                                         start=(hh == 0), stop=(hh == NHEAD - 1))
                                )(dcq, half, hh, bank),
                                waits=mmw if hh == 0 else (),
                                inc="pe" if hh == NHEAD - 1 else None)
                    vw = [("pe", pv), (f"rxl{slot}", lx),
                          xo_prev_store[slot]]
                    dv = em("vector", (lambda slot, bank: lambda v: v.tensor_add(
                            xo_b[slot][:, :], xi_b[slot][:, :], PS[bank][:, :])
                            )(slot, bank), waits=vw, inc="dve")
                    bfree_ffn[bank] = ("dve", dv)
                    xi_prev_add[slot] = dv
                    sv = em("sync", (lambda dcq, half, slot: lambda s: s.dma_start(
                            out=xdram[dcq, :, half * 512: half * 512 + 512],
                            in_=xo_b[slot][:, :]))(dcq, half, slot),
                            waits=[("dve", dv)], inc=f"rxs{slot}", amt=16)
                    rmw[idx] = (dv, (f"rxs{slot}", sv))
                    xo_prev_store[slot] = (f"rxs{slot}", sv)
                    store_of_chunk[(dcq, half)] = (f"rxs{slot}", sv)
                wslot_free[dcq % 2] = S_.cnt["pe"]
                if dcq + 2 < DC:
                    wl[dcq + 2] = emit_wload(dcq % 2, T["wo"][dcq + 2])

        h1, st1 = emit_ln(T["xin"], stats=True, src_is_x=False)
        emit_ffn_call(lambda: emit_gate(st1, "pg1"), h1, T["xin"], xdram,
                      out if stages == 1 else xdram)
        if stages >= 2:
            h2, _ = emit_ln(xdram, stats=False, src_is_x=True)
            emit_attention(h2)
            if stages == 2:
                for dcq in range(DC):
                    w0 = store_of_chunk.get((dcq, 0))
                    w1_ = store_of_chunk.get((dcq, 1))
                    em("sync", (lambda dcq: lambda s: s.dma_start(
                        out=out[dcq], in_=xdram[dcq]))(dcq),
                        waits=[w0, w1_], inc="ost", amt=16)
                em("sync", lambda s: s.nop(), waits=[("ost", S_.cnt["ost"])])
        if stages >= 3:
            h3, st3 = emit_ln(xdram, stats=True, src_is_x=True)
            emit_ffn_call(lambda: emit_gate(st3, "pg3"), h3, xdram, xdram, out)

        for slot in range(2):
            n = S_.cnt.get(f"rxs{slot}", 0)
            if n > 0:
                em("sync", lambda s: s.nop(), waits=[(f"rxs{slot}", n)])

        block = E(nc.Block())
        S_.install(block)

    return nc


def _stage_inputs(L0_attn_x, L0_attn_wq, L0_attn_wk, L0_attn_wv, L0_attn_wo,
                  L0_ffn_gate_w, L0_ffn_gate_b, L0_ffn_w1, L0_ffn_b1,
                  L0_ffn_w2, L0_ffn_b2):
    f32 = np.float32
    w1s = np.ascontiguousarray(
        np.asarray(L0_ffn_w1, f32).reshape(NE, DC, 128, DC, 128)
        .transpose(0, 3, 2, 1, 4).reshape(NE, DC, 128, D))
    w2s = np.ascontiguousarray(
        np.asarray(L0_ffn_w2, f32).reshape(NE, DC, 128, DC, 128)
        .transpose(0, 3, 2, 1, 4).reshape(NE, DC, 128, D))
    wqs = np.ascontiguousarray(
        np.asarray(L0_attn_wq, f32).reshape(DC, 128, NHEAD, 128)
        .transpose(2, 1, 0, 3).reshape(NHEAD, 128, D))
    wks = np.ascontiguousarray(
        np.asarray(L0_attn_wk, f32).reshape(DC, 128, HD)
        .transpose(1, 0, 2).reshape(128, D))
    wvs = np.ascontiguousarray(
        np.asarray(L0_attn_wv, f32).reshape(DC, 128, HD)
        .transpose(1, 0, 2).reshape(128, D))
    wos = np.ascontiguousarray(
        np.asarray(L0_attn_wo, f32).reshape(NHEAD, 128, DC, 128)
        .transpose(2, 1, 0, 3).reshape(DC, 128, D))
    gws = np.ascontiguousarray(
        np.asarray(L0_ffn_gate_w, f32)[0:D, :].reshape(DC, 128, NE)
        .transpose(1, 0, 2).reshape(128, DC * NE))
    b1s = np.ascontiguousarray(
        np.asarray(L0_ffn_b1, f32).reshape(NE, DC, 128)
        .transpose(2, 0, 1).reshape(128, NE * DC))
    b2s = np.ascontiguousarray(np.asarray(L0_ffn_b2, f32))
    smalls = np.zeros((1, 16), f32)
    smalls[0, 0:3] = np.asarray(L0_ffn_gate_w, f32)[D, :]
    smalls[0, 3:6] = np.asarray(L0_ffn_gate_b, f32)
    smalls[0, 6] = 1.0 / (VIS * D)
    smalls[0, 7] = 1.0 / ((S - VIS) * D)
    onesf = np.ones((1, 128), f32)
    onesrow = np.ones((1, 512), f32)
    ones2d = np.ones((128, 128), f32)
    identf = np.eye(128, dtype=f32)

    x = np.asarray(L0_attn_x, f32)
    in_maps = []
    for c in range(8):
        b, parity = c // 2, c % 2
        blocks = [2 * j + parity for j in range(8)]
        tok = np.concatenate([np.arange(g * 128, (g + 1) * 128) for g in blocks])
        xs = np.ascontiguousarray(
            x[b][tok].reshape(TPC, DC, 128).transpose(1, 2, 0))
        vm = (tok < VIS).astype(f32).reshape(1, TPC)
        zm = np.zeros((2, 128, 512), f32)
        for tsel in range(2):
            off = parity * 128 + tsel * 256
            kcol = np.arange(512)[None, :]
            rows = np.arange(128)[:, None]
            zm[tsel] = np.where(kcol < off + rows + 1, 0.0, -1e30)
        kk = np.arange(128)[:, None]
        qq = np.arange(128)[None, :]
        tri = (kk <= qq).astype(f32)
        pm = np.zeros((2, 128, 128), f32)
        pm[0] = tri if parity == 0 else np.ones((128, 128), f32)
        pm[1] = np.zeros((128, 128), f32) if parity == 0 else tri
        in_maps.append(dict(
            xin=xs, w1=w1s, w2=w2s, wq=wqs, wk=wks, wv=wvs, wo=wos,
            gw=gws, b1=b1s, b2=b2s, smalls=smalls, vmask=vm,
            zmask=zm, ptmask=pm, onesf=onesf, onesrow=onesrow,
            ones2d=ones2d, identf=identf))
    return in_maps


def kernel(**inputs):
    in_maps = _stage_inputs(**inputs)
    nc = build_program(debug=_DEBUG, stages=_STAGES)
    res = run_bass_kernel_spmd(nc, in_maps, core_ids=list(range(8)))
    outp = np.zeros((B, S, D), np.float32)
    for c in range(8):
        b, parity = c // 2, c % 2
        arr = res.results[c]["out"]
        tokmaj = arr.transpose(2, 0, 1).reshape(TPC, D)
        for j in range(8):
            g = 2 * j + parity
            outp[b, g * 128:(g + 1) * 128, :] = tokmaj[j * 128:(j + 1) * 128, :]
    return outp


# version 04 (final)
# speedup vs baseline: 1.2463x; 1.2463x over previous
import numpy as np
import ml_dtypes
from contextlib import ExitStack

import concourse.bass as bass
import concourse.mybir as mybir
from concourse.bass_utils import run_bass_kernel_spmd

F32 = mybir.dt.float32
BF16 = mybir.dt.bfloat16
AF = mybir.ActivationFunctionType
ALU = mybir.AluOpType
AXX = mybir.AxisListType.X

B, S, D = 4, 2048, 4096
DC = 32
TPC = 1024
NE = 3
NHEAD = 32
HD = 128
VIS = 614
CSHIFT = 12.0
EPS_LN = 1e-5
QSCALE = 1.0 / float(np.sqrt(HD))
CNT = [1, 1, 2, 2, 3, 3, 4, 4]
ACTF = [AF.Tanh, AF.Gelu, AF.Silu]

_DEBUG = False
_STAGES = 3


class Sched:
    ENGINES = ("tensor", "vector", "scalar", "gpsimd", "sync")

    def __init__(self, nc, ctx):
        self.nc = nc
        self.ctx = ctx
        self.ops = {e: [] for e in self.ENGINES}
        self.cnt = {}
        self.sems = {}
        self.pending = {e: [] for e in self.ENGINES}
        self.scope = "init"

    def sem(self, name):
        if name not in self.sems:
            self.sems[name] = self.ctx.enter_context(self.nc.semaphore(name))
            self.cnt[name] = 0
        return self.sems[name]

    def barrier(self, sems=("pe", "act", "dve")):
        waits = [(s, self.cnt.get(s, 0)) for s in sems if self.cnt.get(s, 0) > 0]
        for e in self.ENGINES:
            self.pending[e] = list(waits)

    def emit(self, eng, fn, waits=(), inc=None, amt=1):
        val = None
        if inc is not None:
            self.sem(inc)
            self.cnt[inc] += amt
            val = self.cnt[inc]
        ws = list(self.pending[eng])
        self.pending[eng] = []
        for w in waits:
            if w is None:
                continue
            s, v = w
            if v and v > 0:
                self.sem(s)
                ws.append((s, v))
        self.ops[eng].append((fn, tuple(ws), inc, amt, self.scope))
        return val

    def install(self, block):
        nc = self.nc
        def mk(eng_name):
            def body(eng):
                seen = {}
                cur = None
                for fn, waits, inc, amt, scope in self.ops[eng_name]:
                    if scope != cur:
                        if cur is not None:
                            nc.leave_named_scope(cur[0], cur[1], False)
                        sid, _ = nc.enter_named_scope(scope, False)
                        cur = (scope, sid)
                    for (s, v) in waits:
                        if seen.get(s, 0) < v:
                            eng.wait_ge(self.sems[s], v)
                            seen[s] = v
                    inst = fn(eng)
                    if inc is not None:
                        inst.then_inc(self.sems[inc], amt)
                if cur is not None:
                    nc.leave_named_scope(cur[0], cur[1], False)
            return body
        block.tensor(mk("tensor"))
        block.vector(mk("vector"))
        block.scalar(mk("scalar"))
        block.gpsimd(mk("gpsimd"))
        block.sync(mk("sync"))


def build_program(debug=False, stages=3):
    nc = bass.Bass("TRN2", target_bir_lowering=False,
                   disable_frame_to_traceback=not debug)

    T = {}
    def din(name, shape):
        T[name] = nc.dram_tensor(name, shape, F32, kind="ExternalInput").ap()
    din("xin", [DC, 128, TPC])
    din("w1", [NE, DC, 128, D])
    din("w2", [NE, DC, 128, D])
    din("wq", [NHEAD, 128, D])
    din("wk", [128, D])
    din("wv", [128, D])
    din("wo", [DC, 128, D])
    din("gw", [128, DC * NE])
    din("b1", [128, NE * DC])
    din("b2", [NE, D])
    din("smalls", [1, 16])
    din("vmask", [1, TPC])
    din("zmask", [2, 128, 512])
    din("ptmask", [2, 128, 128])
    din("onesf", [1, 128])
    din("onesrow", [1, 512])
    din("ones2d", [128, 128])
    din("identf", [128, 128])

    out = nc.dram_tensor("out", [DC, 128, TPC], F32, kind="ExternalOutput").ap()

    xdram = nc.dram_tensor("xdram", [DC, 128, TPC], F32).ap()
    pay_in = nc.dram_tensor("pay_in", [1, 4104], F32).ap()
    pay_out = nc.dram_tensor("pay_out", [2, 4104], F32).ap()
    kag_in = nc.dram_tensor("kag_in", [128, TPC], BF16).ap()
    kag_out = nc.dram_tensor("kag_out", [256, TPC], BF16).ap()
    vag_in = nc.dram_tensor("vag_in", [TPC, 128], BF16).ap()
    vag_out = nc.dram_tensor("vag_out", [2 * TPC, 128], BF16).ap()
    GROUPS = [[0, 1], [2, 3], [4, 5], [6, 7]]

    with ExitStack() as ctx:
        E = ctx.enter_context
        h_sb = E(nc.sbuf_tensor("h_sb", [128, DC * TPC], BF16))
        a_sb = E(nc.sbuf_tensor("a_sb", [128, DC * TPC], BF16))
        wbuf = [E(nc.sbuf_tensor(f"wbuf{i}", [128, D], BF16)) for i in range(2)]
        crep = [E(nc.sbuf_tensor(f"crep{i}", [128, TPC], BF16)) for i in range(NE)]
        crow = [E(nc.sbuf_tensor(f"crow{i}", [1, TPC], BF16)) for i in range(NE)]
        b2row = E(nc.sbuf_tensor("b2row", [1, 1024], BF16))
        xi_b = [E(nc.sbuf_tensor(f"xi{i}", [128, 512], F32)) for i in range(2)]
        xo_b = [E(nc.sbuf_tensor(f"xo{i}", [128, 512], F32)) for i in range(2)]
        tmp_b = [E(nc.sbuf_tensor(f"tmp{i}", [128, 512], F32)) for i in range(2)]
        xfull = [E(nc.sbuf_tensor(f"xfull{i}", [128, TPC], F32)) for i in range(2)]
        mrep = E(nc.sbuf_tensor("mrep", [128, TPC], F32))
        rsrep = E(nc.sbuf_tensor("rsrep", [128, TPC], F32))
        xbf = [E(nc.sbuf_tensor(f"xbf{i}", [128, TPC], BF16)) for i in range(2)]
        sqbf = [E(nc.sbuf_tensor(f"sqbf{i}", [128, TPC], BF16)) for i in range(2)]
        junk = h_sb[:, 0:512]
        klocal = a_sb[:, 0:1024]
        vloc = [a_sb[:, 1024:1152], a_sb[:, 1152:1280]]
        pt_sl = [h_sb[:, 512:1024], h_sb[:, 1024:1536]]
        ptm_sl = [h_sb[:, 1536:1664], h_sb[:, 1664:1792]]
        aot_sl = [h_sb[:, 1792:1920], h_sb[:, 1920:2048]]
        K_sb = h_sb[:, 2048:4096]
        V_sb = h_sb[:, 4096:6144]
        zp_sb = E(nc.sbuf_tensor("zp", [128, 32], F32))
        z_sb = E(nc.sbuf_tensor("z_sb", [128, 8], F32))
        r_sb = E(nc.sbuf_tensor("r_sb", [128, 8], F32))
        rT_sb = E(nc.sbuf_tensor("rT_sb", [1, TPC], F32))
        ptmask_sb = [E(nc.sbuf_tensor(f"ptmask_sb{i}", [128, 128], BF16)) for i in range(2)]
        gw_bf = E(nc.sbuf_tensor("gw_bf", [128, DC * NE], BF16))
        gw_f = E(nc.sbuf_tensor("gw_f", [128, DC * NE], F32))
        b1_sb = E(nc.sbuf_tensor("b1_sb", [128, NE * DC], F32))
        h0_sb = E(nc.sbuf_tensor("h0_sb", [128, DC], F32))
        h0g_sb = E(nc.sbuf_tensor("h0g_sb", [128, DC], F32))
        stat2 = E(nc.sbuf_tensor("stat2", [1, 2], F32))
        p0s = E(nc.sbuf_tensor("p0s", [1, 2], F32))
        p1s = E(nc.sbuf_tensor("p1s", [1, 2], F32))
        tots = E(nc.sbuf_tensor("tots", [1, 2], F32))
        strg = E(nc.sbuf_tensor("strg", [1, 2], F32))
        mr1 = E(nc.sbuf_tensor("mr1", [1, 1], F32))
        vt1 = E(nc.sbuf_tensor("vt1", [1, 1], F32))
        vt2 = E(nc.sbuf_tensor("vt2", [1, 1], F32))
        row3 = [E(nc.sbuf_tensor(f"row3_{i}", [1, NE], F32)) for i in range(6)]
        gbrow_bf = E(nc.sbuf_tensor("gbrow_bf", [1, NE], BF16))
        ind_rep = [E(nc.sbuf_tensor(f"indrep{i}", [128, NE], F32)) for i in range(2)]
        g_t = E(nc.sbuf_tensor("g_t", [128, NE], F32))
        e_t = E(nc.sbuf_tensor("e_t", [128, NE], F32))
        sc_t = E(nc.sbuf_tensor("sc_t", [128, NE], F32))
        c_t = [E(nc.sbuf_tensor(f"c_t{i}", [128, NE], F32)) for i in range(3)]
        nm1 = E(nc.sbuf_tensor("nm1", [128, 1], F32))
        ssum = E(nc.sbuf_tensor("ssum", [128, 1], F32))
        rinv = E(nc.sbuf_tensor("rinv", [128, 1], F32))
        m1c = E(nc.sbuf_tensor("m1c", [128, 1], F32))
        m3c = E(nc.sbuf_tensor("m3c", [128, 1], F32))
        s2c = E(nc.sbuf_tensor("s2c", [128, 1], F32))
        m2c = E(nc.sbuf_tensor("m2c", [128, 1], F32))
        denc = E(nc.sbuf_tensor("denc", [128, 1], F32))
        den2c = E(nc.sbuf_tensor("den2c", [128, 1], F32))
        dinvc = E(nc.sbuf_tensor("dinvc", [128, 1], F32))
        dinv5 = E(nc.sbuf_tensor("dinv5", [128, 1], F32))
        tv0 = E(nc.sbuf_tensor("tv0", [128, 8], F32))
        tv1 = E(nc.sbuf_tensor("tv1", [128, 8], F32))
        g0m = E(nc.sbuf_tensor("g0m", [1, 1], F32))
        g0n = E(nc.sbuf_tensor("g0n", [1, 1], F32))
        indt = E(nc.sbuf_tensor("indt", [1, NE], F32))
        sml = E(nc.sbuf_tensor("sml", [1, 16], F32))
        vm_sb = E(nc.sbuf_tensor("vm_sb", [1, TPC], BF16))
        onesf_sb = E(nc.sbuf_tensor("onesf_sb", [1, 128], F32))
        ones_bf = E(nc.sbuf_tensor("ones_bf", [128, 128], BF16))
        onesrow_bf = E(nc.sbuf_tensor("onesrow_bf", [1, 512], BF16))
        ident_sb = E(nc.sbuf_tensor("ident_sb", [128, 128], F32))

        PS = [E(nc.psum_tensor(f"PS{i}", [128, 512], F32)) for i in range(8)]

        S_ = Sched(nc, ctx)
        em = S_.emit
        for _s in ("pe", "act", "dve", "rxs0", "rxs1", "rxl0", "rxl1"):
            S_.sem(_s)
        store_of_chunk = {}
        bfree_ffn = {}
        xo_prev_store = {0: None, 1: None}
        xi_prev_add = {0: 0, 1: 0}
        xfull_prev = {0: None, 1: None}
        tmp_prev = {0: 0, 1: 0}

        for fn in [
            lambda s: s.dma_start(out=gw_f[:, :], in_=T["gw"]),
            lambda s: s.dma_start(out=b1_sb[:, :], in_=T["b1"]),
            lambda s: s.dma_start(out=sml[:, :], in_=T["smalls"]),
            lambda s: s.dma_start(out=onesf_sb[:, :], in_=T["onesf"]),
            lambda s: s.dma_start(out=ident_sb[:, :], in_=T["identf"]),
        ]:
            em("sync", fn, inc="aux", amt=16)
        AUX_T = S_.cnt["aux"]
        for fn in [
            lambda g: g.dma_start(out=gw_bf[:, :], in_=T["gw"]),
            lambda g: g.dma_start(out=ones_bf[:, :], in_=T["ones2d"]),
            lambda g: g.dma_start(out=ptmask_sb[0][:, :], in_=T["ptmask"][0]),
            lambda g: g.dma_start(out=ptmask_sb[1][:, :], in_=T["ptmask"][1]),
            lambda g: g.dma_start(out=vm_sb[:, :], in_=T["vmask"]),
            lambda g: g.dma_start(out=onesrow_bf[:, :], in_=T["onesrow"]),
        ]:
            em("gpsimd", fn, inc="auxg", amt=16)
        AUXG_T = S_.cnt["auxg"]

        wslot_free = [0, 0]

        def emit_wload(slot, src_ap):
            v = em("gpsimd", (lambda src_ap, slot: lambda g: g.dma_start(
                out=wbuf[slot][:, :], in_=src_ap))(src_ap, slot),
                waits=[("pe", wslot_free[slot])], inc=f"wb{slot}", amt=16)
            return (f"wb{slot}", v)

        _ln_n = [0]
        def emit_ln(src, stats, src_is_x):
            _ln_n[0] += 1
            S_.scope = f"ln{_ln_n[0]}"
            prev_sq = {}
            pe_vals = {}
            for dc in range(DC):
                slot = dc % 2
                waits = []
                if src_is_x:
                    for half in range(2):
                        w = store_of_chunk.get((dc, half))
                        if w:
                            waits.append(w)
                if dc >= 2:
                    waits.append(("act", prev_sq[dc - 2]))
                else:
                    waits.append(xfull_prev[slot])
                lv = em("sync", (lambda dc, slot: lambda s: s.dma_start(
                    out=xfull[slot][:, :], in_=src[dc]))(dc, slot),
                    waits=waits, inc=f"lnl{slot}", amt=16)
                cw = [(f"lnl{slot}", lv)]
                if dc >= 2:
                    cw.append(("pe", pe_vals[dc - 2]))
                em("scalar", (lambda slot: lambda sc: sc.copy(
                    xbf[slot][:, :], xfull[slot][:, :]))(slot),
                    waits=cw, inc="act")
                sv = em("scalar", (lambda slot: lambda sc: sc.activation(
                    sqbf[slot][:, :], xfull[slot][:, :], AF.Square))(slot),
                    inc="act")
                prev_sq[dc] = sv
                mw = [("act", sv), ("auxg", AUXG_T)]
                for b in range(4):
                    if dc == 0 and bfree_ffn.get(b):
                        mw.append(bfree_ffn[b])
                pv = None
                for half in range(2):
                    em("tensor", (lambda slot, half, dc: lambda t: t.matmul(
                        PS[half][:, :], ones_bf[:, :],
                        xbf[slot][:, half * 512: half * 512 + 512],
                        start=(dc == 0), stop=(dc == DC - 1)))(slot, half, dc),
                        waits=mw if half == 0 else ())
                    pv = em("tensor", (lambda slot, half, dc: lambda t: t.matmul(
                        PS[2 + half][:, :], ones_bf[:, :],
                        sqbf[slot][:, half * 512: half * 512 + 512],
                        start=(dc == 0), stop=(dc == DC - 1)))(slot, half, dc),
                        inc="pe" if half == 1 else None)
                pe_vals[dc] = pv
            STATS_PE = pe_vals[DC - 1]
            v0 = None
            for half in range(2):
                sl = slice(half * 512, half * 512 + 512)
                em("vector", (lambda half, sl: lambda v: v.tensor_scalar(
                    mrep[:, sl], PS[half][:, :], 1.0 / D, None, ALU.mult))(half, sl),
                    waits=[("pe", STATS_PE), ("dve", S_.cnt["dve"]),
                           ("act", S_.cnt["act"])], inc="dve")
                v0 = em("vector", (lambda half, sl: lambda v: v.tensor_scalar(
                    tmp_b[half][:, :], PS[2 + half][:, :], 1.0 / D, None,
                    ALU.mult))(half, sl), waits=[("dve", tmp_prev[half])],
                    inc="dve")
            v1 = None
            for half in range(2):
                sl = slice(half * 512, half * 512 + 512)
                v1 = em("vector", (lambda half, sl: lambda v: v.tensor_mul(
                    xo_b[half][:, :], mrep[:, sl], mrep[:, sl]))(half, sl),
                    waits=[("dve", v0)], inc="dve")
            v2 = None
            for half in range(2):
                sl = slice(half * 512, half * 512 + 512)
                v2 = em("vector", (lambda half, sl: lambda v: v.tensor_sub(
                    rsrep[:, sl], tmp_b[half][:, :], xo_b[half][:, :]))(half, sl),
                    waits=[("dve", v1)], inc="dve")
            for b in range(4):
                bfree_ffn[b] = ("dve", v2)
            a1 = None
            for half in range(2):
                sl = slice(half * 512, half * 512 + 512)
                a1 = em("scalar", (lambda half, sl: lambda sc: sc.activation(
                    tmp_b[half][:, :], rsrep[:, sl], AF.Sqrt,
                    bias=np.float32(EPS_LN), scale=np.float32(1.0)))(half, sl),
                    waits=[("dve", v2)], inc="act")
            LNFIN = None
            for half in range(2):
                sl = slice(half * 512, half * 512 + 512)
                LNFIN = em("vector", (lambda half, sl: lambda v: v.reciprocal(
                    rsrep[:, sl], tmp_b[half][:, :]))(half, sl),
                    waits=[("act", a1)], inc="dve")
                tmp_prev[half] = LNFIN

            prev_h = {}
            prev_tail = {}
            prev_abs = {}
            for dc in range(DC):
                slot = dc % 2
                waits = [("act", prev_sq[DC - 1])] if dc < 2 else \
                        [("dve", prev_h[dc - 2])]
                lv = em("sync", (lambda dc, slot: lambda s: s.dma_start(
                    out=xfull[slot][:, :], in_=src[dc]))(dc, slot),
                    waits=waits, inc=f"lnl{slot}", amt=16)
                sw = [(f"lnl{slot}", lv), ("dve", LNFIN)]
                if dc >= 1:
                    sw.append(("dve", prev_tail[dc - 1]))
                h_v = None
                for half in range(2):
                    sl = slice(half * 512, half * 512 + 512)
                    sw2 = (sw if half == 0 else [("dve", h_v)])
                    if dc == 0:
                        sw2 = sw2 + [xo_prev_store[half]]
                    sub_v = em("vector", (lambda slot, half, sl: lambda v: v.tensor_sub(
                        xo_b[half][:, :], xfull[slot][:, sl], mrep[:, sl]))(slot, half, sl),
                        waits=sw2, inc="dve")
                    xfull_prev[slot] = ("dve", sub_v)
                    xo_prev_store[half] = None
                    xi_prev_add[half] = max(xi_prev_add[half], 0)
                    h_v = em("vector", (lambda dc, half, sl: lambda v: v.tensor_mul(
                        h_sb[:, dc * TPC + half * 512: dc * TPC + half * 512 + 512],
                        xo_b[half][:, :], rsrep[:, sl]))(dc, half, sl),
                        waits=[("dve", sub_v)], inc="dve")
                prev_h[dc] = h_v
                prev_tail[dc] = h_v
                if stats:
                    h0v = em("vector", (lambda dc: lambda v: v.tensor_scalar(
                        h0_sb[:, dc:dc + 1], xo_b[0][:, 0:1], rsrep[:, 0:1],
                        None, ALU.mult))(dc),
                        waits=[("dve", h_v)], inc="dve")
                    prev_tail[dc] = h0v
                    aw = [("dve", h_v)]
                    if dc >= 2:
                        aw.append(("pe", prev_abs[dc - 2]))
                    ab = em("scalar", (lambda dc, slot: lambda sc: sc.activation(
                        sqbf[slot][:, :], h_sb[:, dc * TPC:(dc + 1) * TPC],
                        AF.Abs))(dc, slot), waits=aw, inc="act")
                    pv = None
                    for half in range(2):
                        mw2 = [("act", ab)]
                        if dc == 0:
                            for b in (4, 5):
                                if bfree_ffn.get(b):
                                    mw2.append(bfree_ffn[b])
                        pv = em("tensor", (lambda slot, half, dc: lambda t: t.matmul(
                            PS[4 + half][:, :], ones_bf[:, :],
                            sqbf[slot][:, half * 512:half * 512 + 512],
                            start=(dc == 0), stop=(dc == DC - 1)))(slot, half, dc),
                            waits=mw2 if half == 0 else (),
                            inc="pe" if half == 1 else None)
                    prev_abs[dc] = pv
            H_DONE = prev_h[DC - 1]
            if not stats:
                return H_DONE, None
            ABS_PE = prev_abs[DC - 1]
            srow = [rT_sb[:, 0:512], rT_sb[:, 512:1024]]
            sr_v = None
            for half in range(2):
                sr_v = em("vector", (lambda half: lambda v: v.tensor_mul(
                    srow[half][:, :], PS[4 + half][0:1, :],
                    vm_sb[:, half * 512: half * 512 + 512]))(half),
                    waits=[("pe", ABS_PE), ("aux", AUX_T)], inc="dve")
            q1 = em("vector", lambda v: v.reduce_sum(vt1[:, :], srow[0][:, :], AXX),
                    waits=[("dve", sr_v)], inc="dve")
            q2 = em("vector", lambda v: v.reduce_sum(vt2[:, :], srow[1][:, :], AXX),
                    inc="dve")
            q3 = em("vector", lambda v: v.tensor_add(stat2[:, 0:1], vt1[:, :], vt2[:, :]),
                    waits=[("dve", q2), ("dve", q1)], inc="dve")
            q4 = em("vector", lambda v: v.reduce_sum(vt1[:, :], PS[4][0:1, :], AXX),
                    waits=[("dve", q3)], inc="dve")
            q5 = em("vector", lambda v: v.reduce_sum(vt2[:, :], PS[5][0:1, :], AXX),
                    inc="dve")
            q6 = em("vector", lambda v: v.tensor_add(strg[:, 0:1], vt1[:, :], vt2[:, :]),
                    waits=[("dve", q5), ("dve", q4)], inc="dve")
            q7 = em("vector", lambda v: v.tensor_sub(stat2[:, 1:2], strg[:, 0:1],
                                                     stat2[:, 0:1]),
                    waits=[("dve", q6)], inc="dve")
            bfree_ffn[4] = ("dve", q5)
            bfree_ffn[5] = ("dve", q5)
            return H_DONE, q7

        def emit_gate(stat_dve, tag):
            S_.scope = f"gate_{tag}"
            em("sync", lambda s: s.dma_start(out=pay_in[0:1, 0:2], in_=stat2[:, :]),
               waits=[("dve", stat_dve)], inc=tag, amt=16)
            em("sync", lambda s: s.dma_start(
                out=bass.AP(pay_in.tensor, 2, [[1, 1], [32, 128], [1, 32]]),
                in_=h0_sb[:, :]), waits=[("dve", stat_dve)], inc=tag, amt=16)
            em("sync", lambda s: s.dma_start(
                out=pay_in[0:1, 4098:4104], in_=sml[:, 8:14]),
                waits=[("aux", AUX_T)], inc=tag, amt=16)
            cc = em("gpsimd", lambda g: g.collective_compute(
                "AllGather", ALU.bypass, replica_groups=GROUPS,
                ins=[pay_in[:, :]], outs=[pay_out[:, :]]),
                waits=[(tag, S_.cnt[tag])], inc="cc")
            em("sync", lambda s: s.dma_start(out=p0s[:, :], in_=pay_out[0:1, 0:2]),
               waits=[("cc", cc)], inc=tag, amt=16)
            em("sync", lambda s: s.dma_start(out=p1s[:, :], in_=pay_out[1:2, 0:2]),
               inc=tag, amt=16)
            em("sync", lambda s: s.dma_start(
                out=h0g_sb[:, :],
                in_=bass.AP(pay_out.tensor, 2, [[1, 1], [32, 128], [1, 32]])),
                inc=tag, amt=16)
            PAY_T = S_.cnt[tag]
            d1 = em("vector", lambda v: v.tensor_add(tots[:, :], p0s[:, :], p1s[:, :]),
                    waits=[(tag, PAY_T)], inc="dve")
            d2 = em("vector", lambda v: v.tensor_mul(strg[:, :], tots[:, :],
                                                     sml[:, 6:8]),
                    waits=[("dve", d1), ("aux", AUX_T)], inc="dve")
            d3 = em("vector", lambda v: v.tensor_scalar(vt1[:, :], strg[:, 0:1],
                                                        1e-8, None, ALU.add),
                    waits=[("dve", d2)], inc="dve")
            d4 = em("vector", lambda v: v.reciprocal(vt2[:, :], vt1[:, :]),
                    waits=[("dve", d3)], inc="dve")
            d5 = em("vector", lambda v: v.tensor_mul(mr1[:, :], strg[:, 1:2],
                                                     vt2[:, :]),
                    waits=[("dve", d4)], inc="dve")
            gv = None
            for kc in range(DC):
                mw = [(tag, PAY_T), ("aux", AUX_T)]
                if kc == 0 and bfree_ffn.get(6):
                    mw.append(bfree_ffn[6])
                gv = em("tensor", (lambda kc: lambda t: t.matmul(
                    PS[6][0:1, 0:NE], h0g_sb[:, kc:kc + 1],
                    gw_f[:, kc * NE:(kc + 1) * NE],
                    start=(kc == 0), stop=(kc == DC - 1)))(kc),
                    waits=mw if kc == 0 else (),
                    inc="pe" if kc == DC - 1 else None)
            e1 = em("vector", lambda v: v.tensor_scalar(row3[0][:, :], sml[:, 0:3],
                                                        mr1[:, 0:1], None, ALU.mult),
                    waits=[("dve", d5)], inc="dve")
            e2 = em("vector", lambda v: v.tensor_add(row3[1][:, :], row3[0][:, :],
                                                     sml[:, 3:6]),
                    waits=[("dve", e1)], inc="dve")
            e3 = em("vector", lambda v: v.tensor_add(row3[2][:, :], row3[1][:, :],
                                                     PS[6][0:1, 0:NE]),
                    waits=[("dve", e2), ("pe", gv)], inc="dve")
            gb = em("scalar", lambda sc: sc.copy(gbrow_bf[:, :], row3[1][:, :]),
                    waits=[("dve", e2)], inc="act")
            f1 = em("vector", lambda v: v.reduce_max(g0m[:, :], row3[2][:, :], AXX),
                    waits=[("dve", e3)], inc="dve")
            f2 = em("vector", lambda v: v.tensor_reduce(g0n[:, :], row3[2][:, :],
                                                        AXX, ALU.min),
                    waits=[("dve", e3)], inc="dve")
            f3 = em("vector", lambda v: v.tensor_scalar(row3[3][:, :], row3[2][:, :],
                                                        g0m[:, 0:1], None, ALU.is_ge),
                    waits=[("dve", f1)], inc="dve")
            f4 = em("vector", lambda v: v.tensor_scalar(row3[4][:, :], row3[2][:, :],
                                                        g0n[:, 0:1], None, ALU.is_le),
                    waits=[("dve", f2)], inc="dve")
            f5 = em("vector", lambda v: v.tensor_add(indt[:, :], row3[3][:, :],
                                                     row3[4][:, :]),
                    waits=[("dve", f4), ("dve", f3)], inc="dve")
            f6 = em("vector", lambda v: v.tensor_scalar(row3[5][:, :], indt[:, :],
                                                        -1.0, None, ALU.mult),
                    waits=[("dve", f5)], inc="dve")
            f7 = em("vector", lambda v: v.tensor_scalar(indt[:, :], row3[5][:, :],
                                                        1.0, None, ALU.add),
                    waits=[("dve", f6)], inc="dve")
            i1 = em("tensor", lambda t: t.matmul(PS[6][:, 0:NE], onesf_sb[:, :],
                                                 row3[3][:, :], start=True, stop=True),
                    waits=[("dve", f3), ("dve", e3)], inc="pe")
            i1c = em("scalar", lambda sc: sc.copy(ind_rep[0][:, :], PS[6][:, 0:NE]),
                     waits=[("pe", i1)], inc="act")
            i2 = em("tensor", lambda t: t.matmul(PS[6][:, 0:NE], onesf_sb[:, :],
                                                 indt[:, :], start=True, stop=True),
                    waits=[("dve", f7), ("act", i1c)], inc="pe")
            i2c = em("scalar", lambda sc: sc.copy(ind_rep[1][:, :], PS[6][:, 0:NE]),
                     waits=[("pe", i2)], inc="act")

            prev_ge = None
            for tt in range(8):
                mw = [("auxg", AUXG_T)]
                if prev_ge is not None:
                    mw.append(("act", prev_ge))
                elif bfree_ffn.get(7):
                    mw.append(bfree_ffn[7])
                for kc in range(DC):
                    em("tensor", (lambda tt, kc: lambda t: t.matmul(
                        PS[7][:, 0:NE],
                        h_sb[:, kc * TPC + tt * 128: kc * TPC + tt * 128 + 128],
                        gw_bf[:, kc * NE:(kc + 1) * NE],
                        start=(kc == 0), stop=False))(tt, kc),
                        waits=mw if kc == 0 else ())
                gpe = em("tensor", lambda t: t.matmul(
                    PS[7][:, 0:NE], ones_bf[0:1, :], gbrow_bf[:, :],
                    start=False, stop=True),
                    waits=[("act", gb)], inc="pe")
                ge = em("scalar", lambda sc: sc.copy(g_t[:, :], PS[7][:, 0:NE]),
                        waits=[("pe", gpe), ("dve", S_.cnt["dve"])], inc="act")
                prev_ge = ge
                s1 = em("vector", lambda v: v.reduce_max(nm1[:, :], g_t[:, :], AXX,
                                                         negate=True),
                        waits=[("act", ge)], inc="dve")
                s2 = em("scalar", lambda sc: sc.activation(e_t[:, :], g_t[:, :],
                                                           AF.Exp, bias=nm1[:, 0:1],
                                                           scale=np.float32(1.0),
                                                           accum_out=ssum[:, :]),
                        waits=[("dve", s1)], inc="act")
                s3 = em("vector", lambda v: v.reciprocal(rinv[:, :], ssum[:, :]),
                        waits=[("act", s2)], inc="dve")
                s4 = em("vector", lambda v: v.tensor_scalar(sc_t[:, :], e_t[:, :],
                                                            rinv[:, 0:1], None,
                                                            ALU.mult),
                        waits=[("dve", s3)], inc="dve")
                u1 = em("vector", lambda v: v.reduce_max(m1c[:, :], sc_t[:, :], AXX),
                        waits=[("dve", s4)], inc="dve")
                u2 = em("vector", lambda v: v.tensor_reduce(m3c[:, :], sc_t[:, :],
                                                            AXX, ALU.min),
                        waits=[("dve", s4)], inc="dve")
                u3 = em("vector", lambda v: v.reduce_sum(s2c[:, :], sc_t[:, :], AXX),
                        waits=[("dve", s4)], inc="dve")
                u4 = em("vector", lambda v: v.tensor_sub(denc[:, :], s2c[:, :],
                                                         m3c[:, :]),
                        waits=[("dve", u3), ("dve", u2)], inc="dve")
                u5 = em("vector", lambda v: v.tensor_sub(m2c[:, :], denc[:, :],
                                                         m1c[:, :]),
                        waits=[("dve", u4), ("dve", u1)], inc="dve")
                u6 = em("vector", lambda v: v.tensor_scalar(den2c[:, :], denc[:, :],
                                                            1e-8, None, ALU.add),
                        waits=[("dve", u4)], inc="dve")
                u7 = em("vector", lambda v: v.reciprocal(dinvc[:, :], den2c[:, :]),
                        waits=[("dve", u6)], inc="dve")
                u8 = em("vector", lambda v: v.tensor_scalar(dinv5[:, :], dinvc[:, :],
                                                            0.5, None, ALU.mult),
                        waits=[("dve", u7)], inc="dve")
                u9 = em("vector", (lambda tt: lambda v: v.tensor_mul(
                    tv0[:, tt:tt + 1], m1c[:, :], dinv5[:, :]))(tt),
                    waits=[("dve", u8)], inc="dve")
                u10 = em("vector", (lambda tt: lambda v: v.tensor_mul(
                    tv1[:, tt:tt + 1], m2c[:, :], dinv5[:, :]))(tt),
                    waits=[("dve", u8), ("dve", u5)], inc="dve")
                c1 = em("vector", (lambda tt: lambda v: v.tensor_scalar(
                    c_t[0][:, :], ind_rep[0][:, :], tv0[:, tt:tt + 1], None,
                    ALU.mult))(tt),
                    waits=[("dve", u9), ("act", i1c)], inc="dve")
                c2 = em("vector", (lambda tt: lambda v: v.tensor_scalar(
                    c_t[1][:, :], ind_rep[1][:, :], tv1[:, tt:tt + 1], None,
                    ALU.mult))(tt),
                    waits=[("dve", u10), ("act", i2c)], inc="dve")
                c3 = em("vector", lambda v: v.tensor_add(c_t[2][:, :], c_t[0][:, :],
                                                         c_t[1][:, :]),
                        waits=[("dve", c2), ("dve", c1)], inc="dve")
                for e in range(NE):
                    tm = em("tensor", (lambda e: lambda t: t.matmul(
                        PS[6][0:1, 0:128], c_t[2][:, e:e + 1], ident_sb[:, :],
                        start=True, stop=True))(e),
                        waits=[("dve", c3), ("act", S_.cnt["act"])], inc="pe")
                    em("scalar", (lambda tt, e: lambda sc: sc.copy(
                        crow[e][:, tt * 128:(tt + 1) * 128],
                        PS[6][0:1, 0:128]))(tt, e),
                        waits=[("pe", tm)], inc="act")
            crep_last = None
            for e in range(NE):
                for half in range(2):
                    bank = 6 + half
                    rm = em("tensor", (lambda e, half, bank: lambda t: t.matmul(
                        PS[bank][:, :], ones_bf[0:1, :],
                        crow[e][:, half * 512: half * 512 + 512],
                        start=True, stop=True))(e, half, bank),
                        waits=[("act", S_.cnt["act"])], inc="pe")
                    crep_last = em("scalar", (lambda e, half, bank: lambda sc: sc.copy(
                        crep[e][:, half * 512: half * 512 + 512],
                        PS[bank][:, :]))(e, half, bank),
                        waits=[("pe", rm)], inc="act")
            bfree_ffn[6] = ("act", crep_last)
            bfree_ffn[7] = ("act", crep_last)
            return crep_last

        def emit_ffn_call(gate_fn, h_dve, x_src, x_dst, final_dst):
            crep_act = None
            for e in range(NE):
                S_.scope = f"ffn{'1' if x_dst is not final_dst or final_dst is xdram else '?'}_e{e}A" if False else f"ffn_e{e}A_{id(x_src) % 97}"
                wl = {0: emit_wload(0, T["w1"][e, 0]), 1: emit_wload(1, T["w1"][e, 1])}
                act_done = {}
                for fc in range(DC):
                    wsem, wval = wl[fc]
                    for half in range(2):
                        bank = 2 * (fc % 2) + half
                        mmw = [(wsem, wval)]
                        if bfree_ffn.get(bank):
                            mmw.append(bfree_ffn[bank])
                        if fc == 0 and half == 0:
                            mmw += [("dve", h_dve), ("aux", AUX_T)]
                        pv = None
                        for kc in range(DC):
                            pv = em("tensor", (lambda fc, half, kc, bank: lambda t:
                                    t.matmul(PS[bank][:, :],
                                             wbuf[fc % 2][:, kc * 128:(kc + 1) * 128],
                                             h_sb[:, kc * TPC + half * 512:
                                                  kc * TPC + half * 512 + 512],
                                             start=(kc == 0), stop=(kc == DC - 1))
                                    )(fc, half, kc, bank),
                                    waits=mmw if kc == 0 else (),
                                    inc="pe" if kc == DC - 1 else None)
                        av = em("scalar", (lambda fc, half, e, bank: lambda sc:
                                sc.activation(a_sb[:, fc * TPC + half * 512:
                                                   fc * TPC + half * 512 + 512],
                                              PS[bank][:, :], ACTF[e],
                                              bias=b1_sb[:, e * DC + fc:e * DC + fc + 1],
                                              scale=np.float32(1.0)))(fc, half, e, bank),
                                waits=[("pe", pv)], inc="act")
                        bfree_ffn[bank] = ("act", av)
                        act_done[(fc, half)] = av
                    wslot_free[fc % 2] = S_.cnt["pe"]
                    if fc + 2 < DC:
                        wl[fc + 2] = emit_wload(fc % 2, T["w1"][e, fc + 2])
                ACT_H = [act_done[(DC - 1, 0)], act_done[(DC - 1, 1)]]
                if e == 0 and gate_fn is not None:
                    crep_act = gate_fn()
                elif crep_act is None:
                    crep_act = gate_fn
                S_.scope = f"ffn_e{e}B_{id(x_src) % 97}"
                wl = {0: emit_wload(0, T["w2"][e, 0]), 1: emit_wload(1, T["w2"][e, 1])}
                b2l = em("gpsimd", (lambda e: lambda g: g.dma_start(
                    out=b2row[:, :], in_=T["b2"][e:e + 1, 0:1024]))(e),
                    waits=[("pe", wslot_free[0])], inc="b2l", amt=16)
                rmw = {}
                dst = final_dst if e == NE - 1 else x_dst
                rsrc = x_src if e == 0 else x_dst
                for dcq in range(DC):
                    if dcq in (8, 16, 24):
                        b2l = em("gpsimd", (lambda e, dcq: lambda g: g.dma_start(
                            out=b2row[:, :],
                            in_=T["b2"][e:e + 1, dcq * 128:(dcq + 8) * 128]))(e, dcq),
                            waits=[("pe", S_.cnt["pe"])], inc="b2l", amt=16)
                    wsem, wval = wl[dcq]
                    for half in range(2):
                        bank = 4 + 2 * (dcq % 2) + half
                        idx = dcq * 2 + half
                        slot = idx % 2
                        xw = [store_of_chunk.get((dcq, half)),
                              ("dve", xi_prev_add[slot])]
                        lx = em("sync", (lambda dcq, half, slot, rsrc: lambda s:
                                s.dma_start(out=xi_b[slot][:, :],
                                            in_=rsrc[dcq, :, half * 512:
                                                     half * 512 + 512])
                                )(dcq, half, slot, rsrc),
                                waits=xw, inc=f"rxl{slot}", amt=16)
                        mmw = [(wsem, wval), ("act", ACT_H[half]),
                               ("b2l", b2l), ("auxg", AUXG_T)]
                        if bfree_ffn.get(bank):
                            mmw.append(bfree_ffn[bank])
                        em("tensor", (lambda e, dcq, half, bank: lambda t: t.matmul(
                            PS[bank][:, :],
                            b2row[:, (dcq % 8) * 128:(dcq % 8) * 128 + 128],
                            onesrow_bf[0:1, :],
                            start=True, stop=False))(e, dcq, half, bank),
                            waits=mmw)
                        pv = None
                        for fc in range(DC):
                            pv = em("tensor", (lambda fc, dcq, half, bank: lambda t:
                                    t.matmul(PS[bank][:, :],
                                             wbuf[dcq % 2][:, fc * 128:(fc + 1) * 128],
                                             a_sb[:, fc * TPC + half * 512:
                                                  fc * TPC + half * 512 + 512],
                                             start=False, stop=(fc == DC - 1))
                                    )(fc, dcq, half, bank),
                                    inc="pe" if fc == DC - 1 else None)
                        mw2 = [("pe", pv), ("act", crep_act),
                               ("dve", tmp_prev[half])]
                        mu = em("vector", (lambda half, bank, e: lambda v:
                                v.tensor_mul(tmp_b[half][:, :], PS[bank][:, :],
                                             crep[e][:, half * 512:half * 512 + 512])
                                )(half, bank, e), waits=mw2, inc="dve")
                        bfree_ffn[bank] = ("dve", mu)
                        vw = [("dve", mu), (f"rxl{slot}", lx),
                              xo_prev_store[slot]]
                        dv = em("vector", (lambda slot, half: lambda v: v.tensor_add(
                                xo_b[slot][:, :], xi_b[slot][:, :], tmp_b[half][:, :])
                                )(slot, half), waits=vw, inc="dve")
                        tmp_prev[half] = dv
                        xi_prev_add[slot] = dv
                        sv = em("sync", (lambda dcq, half, slot, dst: lambda s:
                                s.dma_start(out=dst[dcq, :, half * 512:half * 512 + 512],
                                            in_=xo_b[slot][:, :]))(dcq, half, slot, dst),
                                waits=[("dve", dv)], inc=f"rxs{slot}", amt=16)
                        rmw[idx] = (dv, (f"rxs{slot}", sv))
                        xo_prev_store[slot] = (f"rxs{slot}", sv)
                        store_of_chunk[(dcq, half)] = (f"rxs{slot}", sv)
                    wslot_free[dcq % 2] = S_.cnt["pe"]
                    if dcq + 2 < DC:
                        wl[dcq + 2] = emit_wload(dcq % 2, T["w2"][e, dcq + 2])

        def emit_attention(h_dve):
            S_.scope = "attn_kv"
            S_.barrier()
            wk_l = emit_wload(0, T["wk"])
            wv_l = emit_wload(1, T["wv"])
            kev = None
            for half in range(2):
                pv = None
                for kc in range(DC):
                    mw = [wk_l, ("dve", h_dve)]
                    if kc == 0 and bfree_ffn.get(half):
                        mw.append(bfree_ffn[half])
                    pv = em("tensor", (lambda half, kc: lambda t: t.matmul(
                        PS[half][:, :], wbuf[0][:, kc * 128:(kc + 1) * 128],
                        h_sb[:, kc * TPC + half * 512: kc * TPC + half * 512 + 512],
                        start=(kc == 0), stop=(kc == DC - 1)))(half, kc),
                        waits=mw if kc == 0 else (),
                        inc="pe" if kc == DC - 1 else None)
                kev = em("scalar", (lambda half: lambda sc: sc.copy(
                    klocal[:, half * 512: half * 512 + 512], PS[half][:, :]))(half),
                    waits=[("pe", pv)], inc="act")
                bfree_ffn[half] = ("act", kev)
            em("sync", lambda s: s.dma_start(out=kag_in[:, :], in_=klocal[:, :]),
               waits=[("act", kev)], inc="kst", amt=16)
            prev_ve = None
            for g in range(8):
                pv = None
                for kc in range(DC):
                    mw = []
                    if kc == 0:
                        mw = [wv_l, ("dve", h_dve)]
                        if prev_ve is not None:
                            mw.append(("act", prev_ve))
                        elif bfree_ffn.get(2):
                            mw.append(bfree_ffn[2])
                    pv = em("tensor", (lambda g, kc: lambda t: t.matmul(
                        PS[2][:, 0:128],
                        h_sb[:, kc * TPC + g * 128: kc * TPC + g * 128 + 128],
                        wbuf[1][:, kc * 128:(kc + 1) * 128],
                        start=(kc == 0), stop=(kc == DC - 1)))(g, kc),
                        waits=mw, inc="pe" if kc == DC - 1 else None)
                sl = g % 2
                vw = [("pe", pv)]
                if g >= 2:
                    vw.append((f"vst{sl}", 16 * (g // 2)))
                ve = em("scalar", (lambda sl: lambda sc: sc.copy(
                    vloc[sl][:, :], PS[2][:, 0:128]))(sl),
                    waits=vw, inc="act")
                prev_ve = ve
                em("sync", (lambda g, sl: lambda s: s.dma_start(
                    out=vag_in[g * 128:(g + 1) * 128, :], in_=vloc[sl][:, :]))(g, sl),
                    waits=[("act", ve)], inc=f"vst{sl}", amt=16)
            bfree_ffn[2] = ("act", prev_ve)
            KV_ST = [("kst", S_.cnt["kst"]), ("vst0", S_.cnt["vst0"]),
                     ("vst1", S_.cnt["vst1"])]
            em("gpsimd", lambda g: g.collective_compute(
                "AllGather", ALU.bypass, replica_groups=GROUPS,
                ins=[kag_in[:, :]], outs=[kag_out[:, :]]),
                waits=KV_ST, inc="cc")
            cc2 = em("gpsimd", lambda g: g.collective_compute(
                "AllGather", ALU.bypass, replica_groups=GROUPS,
                ins=[vag_in[:, :]], outs=[vag_out[:, :]]),
                inc="cc")

            S_.scope = "attn_q"
            wl = {0: emit_wload(0, T["wq"][0]), 1: emit_wload(1, T["wq"][1])}
            qev = {}
            for hh in range(NHEAD):
                wsem, wval = wl[hh]
                for half in range(2):
                    bank = 4 + 2 * (hh % 2) + half
                    mmw = [(wsem, wval)]
                    if hh >= 2:
                        mmw.append(("act", qev[(hh - 2, half)]))
                    elif bfree_ffn.get(bank):
                        mmw.append(bfree_ffn[bank])
                    pv = None
                    for kc in range(DC):
                        pv = em("tensor", (lambda hh, half, kc, bank: lambda t:
                                t.matmul(PS[bank][:, :],
                                         wbuf[hh % 2][:, kc * 128:(kc + 1) * 128],
                                         h_sb[:, kc * TPC + half * 512:
                                              kc * TPC + half * 512 + 512],
                                         start=(kc == 0), stop=(kc == DC - 1))
                                )(hh, half, kc, bank),
                                waits=mmw if kc == 0 else (),
                                inc="pe" if kc == DC - 1 else None)
                    qw = [("pe", pv)]
                    if hh < 2:
                        qw += KV_ST
                    qv = em("scalar", (lambda hh, half, bank: lambda sc: sc.activation(
                        a_sb[:, hh * TPC + half * 512: hh * TPC + half * 512 + 512],
                        PS[bank][:, :], AF.Copy, bias=0.0,
                        scale=np.float32(QSCALE)))(hh, half, bank),
                        waits=qw, inc="act")
                    qev[(hh, half)] = qv
                wslot_free[hh % 2] = S_.cnt["pe"]
                if hh + 2 < NHEAD:
                    wl[hh + 2] = emit_wload(hh % 2, T["wq"][hh + 2])
            Q_PE = S_.cnt["pe"]
            abank_free = {4: ("act", qev[(NHEAD - 2, 0)]),
                          5: ("act", qev[(NHEAD - 2, 1)]),
                          6: ("act", qev[(NHEAD - 1, 0)]),
                          7: ("act", qev[(NHEAD - 1, 1)])}

            for g in range(16):
                rk, li = g % 2, g // 2
                mw = [("cc", cc2), ("pe", Q_PE)] if g == 0 else ()
                em("sync", (lambda g, rk, li: lambda s: s.dma_start(
                    out=K_sb[:, g * 128:(g + 1) * 128],
                    in_=kag_out[rk * 128:(rk + 1) * 128,
                                li * 128:(li + 1) * 128]))(g, rk, li),
                    waits=mw, inc="kv", amt=16)
                em("sync", (lambda g, rk, li: lambda s: s.dma_start(
                    out=V_sb[:, g * 128:(g + 1) * 128],
                    in_=vag_out[rk * TPC + li * 128: rk * TPC + li * 128 + 128, :]
                    ))(g, rk, li), inc="kv", amt=16)
            KV_RB = S_.cnt["kv"]

            pt_sl4 = [h_sb[:, i * 512:(i + 1) * 512] for i in range(4)]
            ptm_sl2 = [h_sb[:, 6144 + i * 128: 6272 + i * 128] for i in range(4)]
            aot_sl2 = [h_sb[:, 6656 + i * 128: 6784 + i * 128] for i in range(2)]
            S_.scope = "attn_heads"
            bfree = {i: bfree_ffn.get(i) for i in range(4)}
            bfree.update(abank_free)
            abank_free2 = {b2_: [bfree[b2_]] for b2_ in range(4, 8)}
            pt_read = {i: None for i in range(4)}
            ptm_read = {i: None for i in range(4)}
            aot_read = {0: 0, 1: 0}
            rrep_read = None
            rT_write_ok = [None] * 8
            SKEW = 3
            for hh in range(NHEAD):
                for qg in range(2):
                    nkt = 8 if qg == 0 else 16
                    exp_v = {}
                    mask_v = {}
                    zred_last = {}
                    for i in range(nkt + SKEW):
                        if i < nkt:
                            kt = i
                            sb_ = kt % 3
                            slot = kt % 4
                            stw = [("act", qev[(hh, qg)]), bfree[sb_], ("kv", KV_RB)]
                            spv = em("tensor", (lambda hh, qg, kt, sb_: lambda t:
                                    t.matmul(PS[sb_][:, :],
                                             K_sb[:, kt * 128:(kt + 1) * 128],
                                             a_sb[:, hh * TPC + qg * 512:
                                                  hh * TPC + qg * 512 + 512],
                                             start=True, stop=True))(hh, qg, kt, sb_),
                                    waits=stw, inc="pe")
                            pe_v = em("scalar", (lambda sb_, slot: lambda sc:
                                    sc.activation(pt_sl4[slot][:, :], PS[sb_][:, :],
                                                  AF.Exp, bias=np.float32(-CSHIFT),
                                                  scale=np.float32(1.0)))(sb_, slot),
                                    waits=[pt_read[slot], ("pe", spv)], inc="act")
                            exp_v[kt] = pe_v
                            bfree[sb_] = ("act", pe_v)
                            if 8 * qg <= kt < 8 * qg + 8:
                                jj = kt // 2 - 4 * qg
                                msl = kt % 4
                                mask_v[kt] = em("vector",
                                    (lambda slot, kt, jj, msl: lambda v:
                                     v.tensor_mul(ptm_sl2[msl][:, :],
                                                  pt_sl4[slot][:, jj * 128: jj * 128 + 128],
                                                  ptmask_sb[kt % 2][:, :]))(slot, kt, jj, msl),
                                    waits=[("act", pe_v), ("auxg", AUXG_T),
                                           ptm_read[msl]], inc="dve")
                        j2 = i - SKEW
                        if 0 <= j2 < nkt:
                            kt = j2
                            slot = kt % 4
                            use_mask = (8 * qg <= kt < 8 * qg + 8)
                            lastpv = None
                            for jj2 in range(4):
                                jloc = 4 * qg + jj2
                                if 2 * jloc + 1 < kt:
                                    continue
                                bb = 4 + jj2
                                first = (kt == 0)
                                stop_z = (kt == 2 * jloc + 1)
                                if use_mask and jj2 == kt // 2 - 4 * qg:
                                    rhs_ap = (lambda m: lambda: ptm_sl2[m][:, :])(kt % 4)
                                    pw = [("dve", mask_v[kt])]
                                else:
                                    rhs_ap = (lambda s, q: lambda:
                                              pt_sl4[s][:, q * 128:q * 128 + 128])(slot, jj2)
                                    pw = [("act", exp_v[kt])]
                                if first:
                                    pw += abank_free2[bb]
                                if stop_z:
                                    em("tensor", (lambda rhs_ap, bb, kt:
                                            lambda t: t.matmul(
                                                PS[bb][:, 128:256],
                                                ones_bf[:, :],
                                                rhs_ap(), start=False, stop=False))(
                                                rhs_ap, bb, kt),
                                            waits=pw, inc="pe")
                                    lastpv = em("tensor", (lambda rhs_ap, bb, kt:
                                            lambda t: t.matmul(
                                                PS[bb][:, 0:128],
                                                V_sb[:, kt * 128:(kt + 1) * 128],
                                                rhs_ap(), start=False, stop=True))(
                                                rhs_ap, bb, kt),
                                            inc="pe")
                                    zred_last[jj2] = lastpv
                                else:
                                    em("tensor", (lambda rhs_ap, bb, first, kt:
                                            lambda t: t.matmul(
                                                PS[bb][:, 0:128],
                                                V_sb[:, kt * 128:(kt + 1) * 128],
                                                rhs_ap(), start=first, stop=False))(
                                                rhs_ap, bb, first, kt),
                                            waits=pw, inc="pe")
                                    lastpv = em("tensor", (lambda rhs_ap, bb, kt:
                                            lambda t: t.matmul(
                                                PS[bb][:, 128:256],
                                                ones_bf[:, :],
                                                rhs_ap(), start=False, stop=False))(
                                                rhs_ap, bb, kt),
                                            inc="pe")
                                pt_read[slot] = ("pe", lastpv)
                                if use_mask and jj2 == kt // 2 - 4 * qg:
                                    ptm_read[kt % 4] = ("pe", lastpv)
                    for jj2 in range(4):
                        jloc = 4 * qg + jj2
                        bb = 4 + jj2
                        sl = jj2 % 2
                        rv = em("vector", (lambda bb, jloc: lambda v: v.reciprocal(
                            rT_sb[:, jloc * 128:(jloc + 1) * 128],
                            PS[bb][0:1, 128:256]))(bb, jloc),
                            waits=[("pe", zred_last[jj2]),
                                   ("pe", rT_write_ok[jloc])], inc="dve")
                        aev = em("scalar", (lambda bb, sl: lambda sc: sc.copy(
                            aot_sl2[sl][:, :], PS[bb][:, 0:128]))(bb, sl),
                            waits=[("pe", zred_last[jj2]), ("dve", aot_read[sl])],
                            inc="act")
                        abank_free2[bb] = [("act", aev), ("dve", rv)]
                        rmw_ = [("dve", rv), ("aux", AUX_T), ("dve", rrep_read)]
                        rm = em("tensor", (lambda jloc: lambda t: t.matmul(
                            PS[3][:, 0:128], onesf_sb[:, :],
                            rT_sb[:, jloc * 128:(jloc + 1) * 128],
                            start=True, stop=True))(jloc),
                            waits=rmw_, inc="pe")
                        rT_write_ok[jloc] = rm
                        sv = em("vector", (lambda hh, jloc, sl: lambda v:
                                v.tensor_mul(a_sb[:, hh * TPC + jloc * 128:
                                                  hh * TPC + jloc * 128 + 128],
                                             aot_sl2[sl][:, :], PS[3][:, 0:128]))(
                                             hh, jloc, sl),
                                waits=[("pe", rm), ("act", aev)], inc="dve")
                        aot_read[sl] = sv
                        rrep_read = sv
            AODONE = S_.cnt["dve"]
            for b in range(4, 8):
                bfree_ffn[b] = bfree[b]

            S_.scope = "attn_wo"
            S_.barrier()
            wl = {0: emit_wload(0, T["wo"][0]), 1: emit_wload(1, T["wo"][1])}
            rmw = {}
            for dcq in range(DC):
                wsem, wval = wl[dcq]
                for half in range(2):
                    bank = (dcq % 2) * 2 + half
                    idx = dcq * 2 + half
                    slot = idx % 2
                    xw = [store_of_chunk.get((dcq, half)),
                          ("dve", xi_prev_add[slot])]
                    lx = em("sync", (lambda dcq, half, slot: lambda s: s.dma_start(
                        out=xi_b[slot][:, :],
                        in_=xdram[dcq, :, half * 512: half * 512 + 512]))(dcq, half, slot),
                        waits=xw, inc=f"rxl{slot}", amt=16)
                    mmw = [(wsem, wval), ("dve", AODONE)]
                    if idx >= 2:
                        mmw.append(("dve", rmw[idx - 2][0]))
                    pv = None
                    for hh in range(NHEAD):
                        pv = em("tensor", (lambda dcq, half, hh, bank: lambda t:
                                t.matmul(PS[bank][:, :],
                                         wbuf[dcq % 2][:, hh * 128:(hh + 1) * 128],
                                         a_sb[:, hh * TPC + half * 512:
                                              hh * TPC + half * 512 + 512],
                                         start=(hh == 0), stop=(hh == NHEAD - 1))
                                )(dcq, half, hh, bank),
                                waits=mmw if hh == 0 else (),
                                inc="pe" if hh == NHEAD - 1 else None)
                    vw = [("pe", pv), (f"rxl{slot}", lx),
                          xo_prev_store[slot]]
                    dv = em("vector", (lambda slot, bank: lambda v: v.tensor_add(
                            xo_b[slot][:, :], xi_b[slot][:, :], PS[bank][:, :])
                            )(slot, bank), waits=vw, inc="dve")
                    bfree_ffn[bank] = ("dve", dv)
                    xi_prev_add[slot] = dv
                    sv = em("sync", (lambda dcq, half, slot: lambda s: s.dma_start(
                            out=xdram[dcq, :, half * 512: half * 512 + 512],
                            in_=xo_b[slot][:, :]))(dcq, half, slot),
                            waits=[("dve", dv)], inc=f"rxs{slot}", amt=16)
                    rmw[idx] = (dv, (f"rxs{slot}", sv))
                    xo_prev_store[slot] = (f"rxs{slot}", sv)
                    store_of_chunk[(dcq, half)] = (f"rxs{slot}", sv)
                wslot_free[dcq % 2] = S_.cnt["pe"]
                if dcq + 2 < DC:
                    wl[dcq + 2] = emit_wload(dcq % 2, T["wo"][dcq + 2])

        h1, st1 = emit_ln(T["xin"], stats=True, src_is_x=False)
        emit_ffn_call(lambda: emit_gate(st1, "pg1"), h1, T["xin"], xdram,
                      out if stages == 1 else xdram)
        if stages >= 2:
            h2, _ = emit_ln(xdram, stats=False, src_is_x=True)
            emit_attention(h2)
            if stages == 2:
                for dcq in range(DC):
                    w0 = store_of_chunk.get((dcq, 0))
                    w1_ = store_of_chunk.get((dcq, 1))
                    em("sync", (lambda dcq: lambda s: s.dma_start(
                        out=out[dcq], in_=xdram[dcq]))(dcq),
                        waits=[w0, w1_], inc="ost", amt=16)
                em("sync", lambda s: s.nop(), waits=[("ost", S_.cnt["ost"])])
        if stages >= 3:
            h3, st3 = emit_ln(xdram, stats=True, src_is_x=True)
            emit_ffn_call(lambda: emit_gate(st3, "pg3"), h3, xdram, xdram, out)

        for slot in range(2):
            n = S_.cnt.get(f"rxs{slot}", 0)
            if n > 0:
                em("sync", lambda s: s.nop(), waits=[(f"rxs{slot}", n)])

        block = E(nc.Block())
        S_.install(block)

    return nc


def _stage_inputs(L0_attn_x, L0_attn_wq, L0_attn_wk, L0_attn_wv, L0_attn_wo,
                  L0_ffn_gate_w, L0_ffn_gate_b, L0_ffn_w1, L0_ffn_b1,
                  L0_ffn_w2, L0_ffn_b2):
    f32 = np.float32
    w1s = np.ascontiguousarray(
        np.asarray(L0_ffn_w1, f32).reshape(NE, DC, 128, DC, 128)
        .transpose(0, 3, 2, 1, 4).reshape(NE, DC, 128, D))
    w2s = np.ascontiguousarray(
        np.asarray(L0_ffn_w2, f32).reshape(NE, DC, 128, DC, 128)
        .transpose(0, 3, 2, 1, 4).reshape(NE, DC, 128, D))
    wqs = np.ascontiguousarray(
        np.asarray(L0_attn_wq, f32).reshape(DC, 128, NHEAD, 128)
        .transpose(2, 1, 0, 3).reshape(NHEAD, 128, D))
    wks = np.ascontiguousarray(
        np.asarray(L0_attn_wk, f32).reshape(DC, 128, HD)
        .transpose(1, 0, 2).reshape(128, D))
    wvs = np.ascontiguousarray(
        np.asarray(L0_attn_wv, f32).reshape(DC, 128, HD)
        .transpose(1, 0, 2).reshape(128, D))
    wos = np.ascontiguousarray(
        np.asarray(L0_attn_wo, f32).reshape(NHEAD, 128, DC, 128)
        .transpose(2, 1, 0, 3).reshape(DC, 128, D))
    gws = np.ascontiguousarray(
        np.asarray(L0_ffn_gate_w, f32)[0:D, :].reshape(DC, 128, NE)
        .transpose(1, 0, 2).reshape(128, DC * NE))
    b1s = np.ascontiguousarray(
        np.asarray(L0_ffn_b1, f32).reshape(NE, DC, 128)
        .transpose(2, 0, 1).reshape(128, NE * DC))
    b2s = np.ascontiguousarray(np.asarray(L0_ffn_b2, f32))
    smalls = np.zeros((1, 16), f32)
    smalls[0, 0:3] = np.asarray(L0_ffn_gate_w, f32)[D, :]
    smalls[0, 3:6] = np.asarray(L0_ffn_gate_b, f32)
    smalls[0, 6] = 1.0 / (VIS * D)
    smalls[0, 7] = 1.0 / ((S - VIS) * D)
    onesf = np.ones((1, 128), f32)
    onesrow = np.ones((1, 512), f32)
    ones2d = np.ones((128, 128), f32)
    identf = np.eye(128, dtype=f32)

    x = np.asarray(L0_attn_x, f32)
    in_maps = []
    for c in range(8):
        b, parity = c // 2, c % 2
        blocks = [2 * j + parity for j in range(8)]
        tok = np.concatenate([np.arange(g * 128, (g + 1) * 128) for g in blocks])
        xs = np.ascontiguousarray(
            x[b][tok].reshape(TPC, DC, 128).transpose(1, 2, 0))
        vm = (tok < VIS).astype(f32).reshape(1, TPC)
        zm = np.zeros((2, 128, 512), f32)
        for tsel in range(2):
            off = parity * 128 + tsel * 256
            kcol = np.arange(512)[None, :]
            rows = np.arange(128)[:, None]
            zm[tsel] = np.where(kcol < off + rows + 1, 0.0, -1e30)
        kk = np.arange(128)[:, None]
        qq = np.arange(128)[None, :]
        tri = (kk <= qq).astype(f32)
        pm = np.zeros((2, 128, 128), f32)
        pm[0] = tri if parity == 0 else np.ones((128, 128), f32)
        pm[1] = np.zeros((128, 128), f32) if parity == 0 else tri
        in_maps.append(dict(
            xin=xs, w1=w1s, w2=w2s, wq=wqs, wk=wks, wv=wvs, wo=wos,
            gw=gws, b1=b1s, b2=b2s, smalls=smalls, vmask=vm,
            zmask=zm, ptmask=pm, onesf=onesf, onesrow=onesrow,
            ones2d=ones2d, identf=identf))
    return in_maps


def kernel(**inputs):
    in_maps = _stage_inputs(**inputs)
    nc = build_program(debug=_DEBUG, stages=_STAGES)
    res = run_bass_kernel_spmd(nc, in_maps, core_ids=list(range(8)))
    outp = np.zeros((B, S, D), np.float32)
    for c in range(8):
        b, parity = c // 2, c % 2
        arr = res.results[c]["out"]
        tokmaj = arr.transpose(2, 0, 1).reshape(TPC, D)
        for j in range(8):
            g = 2 * j + parity
            outp[b, g * 128:(g + 1) * 128, :] = tokmaj[j * 128:(j + 1) * 128, :]
    return outp


# version 05
# speedup vs baseline: 1.3078x; 1.0493x over previous
import numpy as np
import ml_dtypes
from contextlib import ExitStack

import concourse.bass as bass
import concourse.mybir as mybir
from concourse.bass_utils import run_bass_kernel_spmd

F32 = mybir.dt.float32
BF16 = mybir.dt.bfloat16
AF = mybir.ActivationFunctionType
ALU = mybir.AluOpType
AXX = mybir.AxisListType.X

B, S, D = 4, 2048, 4096
DC = 32
TPC = 1024
NE = 3
NHEAD = 32
HD = 128
VIS = 614
CSHIFT = 12.0
EPS_LN = 1e-5
QSCALE = 1.0 / float(np.sqrt(HD))
CNT = [1, 1, 2, 2, 3, 3, 4, 4]
ACTF = [AF.Tanh, AF.Gelu, AF.Silu]

_DEBUG = False
_STAGES = 3


class Sched:
    ENGINES = ("tensor", "vector", "scalar", "gpsimd", "sync")

    def __init__(self, nc, ctx):
        self.nc = nc
        self.ctx = ctx
        self.ops = {e: [] for e in self.ENGINES}
        self.cnt = {}
        self.sems = {}
        self.pending = {e: [] for e in self.ENGINES}
        self.scope = "init"

    def sem(self, name):
        if name not in self.sems:
            self.sems[name] = self.ctx.enter_context(self.nc.semaphore(name))
            self.cnt[name] = 0
        return self.sems[name]

    def barrier(self, sems=("pe", "act", "dve")):
        waits = [(s, self.cnt.get(s, 0)) for s in sems if self.cnt.get(s, 0) > 0]
        for e in self.ENGINES:
            self.pending[e] = list(waits)

    def emit(self, eng, fn, waits=(), inc=None, amt=1):
        val = None
        if inc is not None:
            self.sem(inc)
            self.cnt[inc] += amt
            val = self.cnt[inc]
        ws = list(self.pending[eng])
        self.pending[eng] = []
        for w in waits:
            if w is None:
                continue
            s, v = w
            if v and v > 0:
                self.sem(s)
                ws.append((s, v))
        self.ops[eng].append((fn, tuple(ws), inc, amt, self.scope))
        return val

    def install(self, block):
        nc = self.nc
        def mk(eng_name):
            def body(eng):
                seen = {}
                cur = None
                for fn, waits, inc, amt, scope in self.ops[eng_name]:
                    if scope != cur:
                        if cur is not None:
                            nc.leave_named_scope(cur[0], cur[1], False)
                        sid, _ = nc.enter_named_scope(scope, False)
                        cur = (scope, sid)
                    for (s, v) in waits:
                        if seen.get(s, 0) < v:
                            eng.wait_ge(self.sems[s], v)
                            seen[s] = v
                    inst = fn(eng)
                    if inc is not None:
                        inst.then_inc(self.sems[inc], amt)
                if cur is not None:
                    nc.leave_named_scope(cur[0], cur[1], False)
            return body
        block.tensor(mk("tensor"))
        block.vector(mk("vector"))
        block.scalar(mk("scalar"))
        block.gpsimd(mk("gpsimd"))
        block.sync(mk("sync"))


def build_program(debug=False, stages=3):
    nc = bass.Bass("TRN2", target_bir_lowering=False,
                   disable_frame_to_traceback=not debug)

    T = {}
    def din(name, shape):
        T[name] = nc.dram_tensor(name, shape, F32, kind="ExternalInput").ap()
    din("xin", [DC, 128, TPC])
    din("w1", [NE, DC, 128, D])
    din("w2", [NE, DC, 128, D])
    din("wq", [NHEAD, 128, D])
    din("wk", [128, D])
    din("wv", [128, D])
    din("wo", [DC, 128, D])
    din("gw", [128, DC * NE])
    din("b1", [128, NE * DC])
    din("b2", [NE, D])
    din("smalls", [1, 16])
    din("vmask", [1, TPC])
    din("zmask", [2, 128, 512])
    din("ptmask", [2, 128, 128])
    din("onesf", [1, 128])
    din("onesrow", [1, 512])
    din("ones2d", [128, 128])
    din("identf", [128, 128])

    out = nc.dram_tensor("out", [DC, 128, TPC], F32, kind="ExternalOutput").ap()

    xdram = nc.dram_tensor("xdram", [DC, 128, TPC], F32).ap()
    pay_in = nc.dram_tensor("pay_in", [1, 4104], F32).ap()
    pay_out = nc.dram_tensor("pay_out", [2, 4104], F32).ap()
    kag_in = nc.dram_tensor("kag_in", [128, TPC], BF16).ap()
    kag_out = nc.dram_tensor("kag_out", [256, TPC], BF16).ap()
    vag_in = nc.dram_tensor("vag_in", [TPC, 128], BF16).ap()
    vag_out = nc.dram_tensor("vag_out", [2 * TPC, 128], BF16).ap()
    GROUPS = [[0, 1], [2, 3], [4, 5], [6, 7]]

    with ExitStack() as ctx:
        E = ctx.enter_context
        h_sb = E(nc.sbuf_tensor("h_sb", [128, DC * TPC], BF16))
        a_sb = E(nc.sbuf_tensor("a_sb", [128, DC * TPC], BF16))
        wbuf = [E(nc.sbuf_tensor(f"wbuf{i}", [128, D], BF16)) for i in range(2)]
        crep = [E(nc.sbuf_tensor(f"crep{i}", [128, TPC], BF16)) for i in range(NE)]
        crow = [E(nc.sbuf_tensor(f"crow{i}", [1, TPC], BF16)) for i in range(NE)]
        b2row = E(nc.sbuf_tensor("b2row", [1, 1024], BF16))
        xi_b = [E(nc.sbuf_tensor(f"xi{i}", [128, 512], F32)) for i in range(3)]
        xo_b = [E(nc.sbuf_tensor(f"xo{i}", [128, 512], F32)) for i in range(2)]
        tmp_b = [E(nc.sbuf_tensor(f"tmp{i}", [128, 512], F32)) for i in range(2)]
        xfull = [E(nc.sbuf_tensor(f"xfull{i}", [128, TPC], F32)) for i in range(2)]
        mrep = E(nc.sbuf_tensor("mrep", [128, TPC], F32))
        rsrep = E(nc.sbuf_tensor("rsrep", [128, TPC], F32))
        xbf = [E(nc.sbuf_tensor(f"xbf{i}", [128, TPC], BF16)) for i in range(2)]
        sqbf = [E(nc.sbuf_tensor(f"sqbf{i}", [128, TPC], BF16)) for i in range(2)]
        junk = h_sb[:, 0:512]
        klocal = a_sb[:, 0:1024]
        vloc = [a_sb[:, 1024:1152], a_sb[:, 1152:1280]]
        pt_sl = [h_sb[:, 512:1024], h_sb[:, 1024:1536]]
        ptm_sl = [h_sb[:, 1536:1664], h_sb[:, 1664:1792]]
        aot_sl = [h_sb[:, 1792:1920], h_sb[:, 1920:2048]]
        K_sb = h_sb[:, 2048:4096]
        V_sb = h_sb[:, 4096:6144]
        zp_sb = E(nc.sbuf_tensor("zp", [128, 32], F32))
        z_sb = E(nc.sbuf_tensor("z_sb", [128, 8], F32))
        r_sb = E(nc.sbuf_tensor("r_sb", [128, 8], F32))
        rT_sb = E(nc.sbuf_tensor("rT_sb", [1, TPC], F32))
        ptmask_sb = [E(nc.sbuf_tensor(f"ptmask_sb{i}", [128, 128], BF16)) for i in range(2)]
        gw_bf = E(nc.sbuf_tensor("gw_bf", [128, DC * NE], BF16))
        gw_f = E(nc.sbuf_tensor("gw_f", [128, DC * NE], F32))
        b1_sb = E(nc.sbuf_tensor("b1_sb", [128, NE * DC], F32))
        h0_sb = E(nc.sbuf_tensor("h0_sb", [128, DC], F32))
        h0g_sb = E(nc.sbuf_tensor("h0g_sb", [128, DC], F32))
        stat2 = E(nc.sbuf_tensor("stat2", [1, 2], F32))
        p0s = E(nc.sbuf_tensor("p0s", [1, 2], F32))
        p1s = E(nc.sbuf_tensor("p1s", [1, 2], F32))
        tots = E(nc.sbuf_tensor("tots", [1, 2], F32))
        strg = E(nc.sbuf_tensor("strg", [1, 2], F32))
        mr1 = E(nc.sbuf_tensor("mr1", [1, 1], F32))
        vt1 = E(nc.sbuf_tensor("vt1", [1, 1], F32))
        vt2 = E(nc.sbuf_tensor("vt2", [1, 1], F32))
        row3 = [E(nc.sbuf_tensor(f"row3_{i}", [1, NE], F32)) for i in range(6)]
        gbrow_bf = E(nc.sbuf_tensor("gbrow_bf", [1, NE], BF16))
        ind_rep = [E(nc.sbuf_tensor(f"indrep{i}", [128, NE], F32)) for i in range(2)]
        g_t = E(nc.sbuf_tensor("g_t", [128, NE], F32))
        e_t = E(nc.sbuf_tensor("e_t", [128, NE], F32))
        sc_t = E(nc.sbuf_tensor("sc_t", [128, NE], F32))
        c_t = [E(nc.sbuf_tensor(f"c_t{i}", [128, NE], F32)) for i in range(3)]
        nm1 = E(nc.sbuf_tensor("nm1", [128, 1], F32))
        ssum = E(nc.sbuf_tensor("ssum", [128, 1], F32))
        rinv = E(nc.sbuf_tensor("rinv", [128, 1], F32))
        m1c = E(nc.sbuf_tensor("m1c", [128, 1], F32))
        m3c = E(nc.sbuf_tensor("m3c", [128, 1], F32))
        s2c = E(nc.sbuf_tensor("s2c", [128, 1], F32))
        m2c = E(nc.sbuf_tensor("m2c", [128, 1], F32))
        denc = E(nc.sbuf_tensor("denc", [128, 1], F32))
        den2c = E(nc.sbuf_tensor("den2c", [128, 1], F32))
        dinvc = E(nc.sbuf_tensor("dinvc", [128, 1], F32))
        dinv5 = E(nc.sbuf_tensor("dinv5", [128, 1], F32))
        tv0 = E(nc.sbuf_tensor("tv0", [128, 8], F32))
        tv1 = E(nc.sbuf_tensor("tv1", [128, 8], F32))
        g0m = E(nc.sbuf_tensor("g0m", [1, 1], F32))
        g0n = E(nc.sbuf_tensor("g0n", [1, 1], F32))
        indt = E(nc.sbuf_tensor("indt", [1, NE], F32))
        sml = E(nc.sbuf_tensor("sml", [1, 16], F32))
        vm_sb = E(nc.sbuf_tensor("vm_sb", [1, TPC], BF16))
        onesf_sb = E(nc.sbuf_tensor("onesf_sb", [1, 128], F32))
        ones_bf = E(nc.sbuf_tensor("ones_bf", [128, 128], BF16))
        onesrow_bf = E(nc.sbuf_tensor("onesrow_bf", [1, 512], BF16))
        ident_sb = E(nc.sbuf_tensor("ident_sb", [128, 128], F32))

        PS = [E(nc.psum_tensor(f"PS{i}", [128, 512], F32)) for i in range(8)]

        S_ = Sched(nc, ctx)
        em = S_.emit
        for _s in ("pe", "act", "dve", "rxs0", "rxs1", "rxl0", "rxl1", "rxl2"):
            S_.sem(_s)
        store_of_chunk = {}
        bfree_ffn = {}
        xo_prev_store = {0: None, 1: None}
        xi_prev_add = {0: 0, 1: 0, 2: 0}
        xfull_prev = {0: None, 1: None}
        tmp_prev = {0: 0, 1: 0}

        for fn in [
            lambda s: s.dma_start(out=gw_f[:, :], in_=T["gw"]),
            lambda s: s.dma_start(out=b1_sb[:, :], in_=T["b1"]),
            lambda s: s.dma_start(out=sml[:, :], in_=T["smalls"]),
            lambda s: s.dma_start(out=onesf_sb[:, :], in_=T["onesf"]),
            lambda s: s.dma_start(out=ident_sb[:, :], in_=T["identf"]),
        ]:
            em("sync", fn, inc="aux", amt=16)
        AUX_T = S_.cnt["aux"]
        for fn in [
            lambda g: g.dma_start(out=gw_bf[:, :], in_=T["gw"]),
            lambda g: g.dma_start(out=ones_bf[:, :], in_=T["ones2d"]),
            lambda g: g.dma_start(out=ptmask_sb[0][:, :], in_=T["ptmask"][0]),
            lambda g: g.dma_start(out=ptmask_sb[1][:, :], in_=T["ptmask"][1]),
            lambda g: g.dma_start(out=vm_sb[:, :], in_=T["vmask"]),
            lambda g: g.dma_start(out=onesrow_bf[:, :], in_=T["onesrow"]),
        ]:
            em("gpsimd", fn, inc="auxg", amt=16)
        AUXG_T = S_.cnt["auxg"]

        wslot_free = [0, 0]

        def emit_wload(slot, src_ap):
            v = em("gpsimd", (lambda src_ap, slot: lambda g: g.dma_start(
                out=wbuf[slot][:, :], in_=src_ap))(src_ap, slot),
                waits=[("pe", wslot_free[slot])], inc=f"wb{slot}", amt=16)
            return (f"wb{slot}", v)

        _ln_n = [0]
        def emit_ln(src, stats, src_is_x):
            _ln_n[0] += 1
            S_.scope = f"ln{_ln_n[0]}"
            prev_sq = {}
            pe_vals = {}
            for dc in range(DC):
                slot = dc % 2
                waits = []
                if src_is_x:
                    for half in range(2):
                        w = store_of_chunk.get((dc, half))
                        if w:
                            waits.append(w)
                if dc >= 2:
                    waits.append(("act", prev_sq[dc - 2]))
                else:
                    waits.append(xfull_prev[slot])
                lv = em("sync", (lambda dc, slot: lambda s: s.dma_start(
                    out=xfull[slot][:, :], in_=src[dc]))(dc, slot),
                    waits=waits, inc=f"lnl{slot}", amt=16)
                cw = [(f"lnl{slot}", lv)]
                if dc >= 2:
                    cw.append(("pe", pe_vals[dc - 2]))
                em("scalar", (lambda slot: lambda sc: sc.copy(
                    xbf[slot][:, :], xfull[slot][:, :]))(slot),
                    waits=cw, inc="act")
                sv = em("scalar", (lambda slot: lambda sc: sc.activation(
                    sqbf[slot][:, :], xfull[slot][:, :], AF.Square))(slot),
                    inc="act")
                prev_sq[dc] = sv
                mw = [("act", sv), ("auxg", AUXG_T)]
                for b in range(4):
                    if dc == 0 and bfree_ffn.get(b):
                        mw.append(bfree_ffn[b])
                pv = None
                for half in range(2):
                    em("tensor", (lambda slot, half, dc: lambda t: t.matmul(
                        PS[half][:, :], ones_bf[:, :],
                        xbf[slot][:, half * 512: half * 512 + 512],
                        start=(dc == 0), stop=(dc == DC - 1)))(slot, half, dc),
                        waits=mw if half == 0 else ())
                    pv = em("tensor", (lambda slot, half, dc: lambda t: t.matmul(
                        PS[2 + half][:, :], ones_bf[:, :],
                        sqbf[slot][:, half * 512: half * 512 + 512],
                        start=(dc == 0), stop=(dc == DC - 1)))(slot, half, dc),
                        inc="pe" if half == 1 else None)
                pe_vals[dc] = pv
            STATS_PE = pe_vals[DC - 1]
            v0 = None
            for half in range(2):
                sl = slice(half * 512, half * 512 + 512)
                em("vector", (lambda half, sl: lambda v: v.tensor_scalar(
                    mrep[:, sl], PS[half][:, :], 1.0 / D, None, ALU.mult))(half, sl),
                    waits=[("pe", STATS_PE), ("dve", S_.cnt["dve"]),
                           ("act", S_.cnt["act"])], inc="dve")
                v0 = em("vector", (lambda half, sl: lambda v: v.tensor_scalar(
                    tmp_b[half][:, :], PS[2 + half][:, :], 1.0 / D, None,
                    ALU.mult))(half, sl), waits=[("dve", tmp_prev[half])],
                    inc="dve")
            v1 = None
            for half in range(2):
                sl = slice(half * 512, half * 512 + 512)
                v1 = em("vector", (lambda half, sl: lambda v: v.tensor_mul(
                    xo_b[half][:, :], mrep[:, sl], mrep[:, sl]))(half, sl),
                    waits=[("dve", v0)], inc="dve")
            v2 = None
            for half in range(2):
                sl = slice(half * 512, half * 512 + 512)
                v2 = em("vector", (lambda half, sl: lambda v: v.tensor_sub(
                    rsrep[:, sl], tmp_b[half][:, :], xo_b[half][:, :]))(half, sl),
                    waits=[("dve", v1)], inc="dve")
            for b in range(4):
                bfree_ffn[b] = ("dve", v2)
            a1 = None
            for half in range(2):
                sl = slice(half * 512, half * 512 + 512)
                a1 = em("scalar", (lambda half, sl: lambda sc: sc.activation(
                    tmp_b[half][:, :], rsrep[:, sl], AF.Sqrt,
                    bias=np.float32(EPS_LN), scale=np.float32(1.0)))(half, sl),
                    waits=[("dve", v2)], inc="act")
            LNFIN = None
            for half in range(2):
                sl = slice(half * 512, half * 512 + 512)
                LNFIN = em("vector", (lambda half, sl: lambda v: v.reciprocal(
                    rsrep[:, sl], tmp_b[half][:, :]))(half, sl),
                    waits=[("act", a1)], inc="dve")
                tmp_prev[half] = LNFIN

            prev_h = {}
            prev_tail = {}
            prev_abs = {}
            for dc in range(DC):
                slot = dc % 2
                waits = [("act", prev_sq[DC - 1])] if dc < 2 else \
                        [("dve", prev_h[dc - 2])]
                lv = em("sync", (lambda dc, slot: lambda s: s.dma_start(
                    out=xfull[slot][:, :], in_=src[dc]))(dc, slot),
                    waits=waits, inc=f"lnl{slot}", amt=16)
                sw = [(f"lnl{slot}", lv), ("dve", LNFIN)]
                if dc >= 1:
                    sw.append(("dve", prev_tail[dc - 1]))
                h_v = None
                for half in range(2):
                    sl = slice(half * 512, half * 512 + 512)
                    sw2 = (sw if half == 0 else [("dve", h_v)])
                    if dc == 0:
                        sw2 = sw2 + [xo_prev_store[half]]
                    sub_v = em("vector", (lambda slot, half, sl: lambda v: v.tensor_sub(
                        xo_b[half][:, :], xfull[slot][:, sl], mrep[:, sl]))(slot, half, sl),
                        waits=sw2, inc="dve")
                    xfull_prev[slot] = ("dve", sub_v)
                    xo_prev_store[half] = None
                    xi_prev_add[half] = max(xi_prev_add[half], 0)
                    h_v = em("vector", (lambda dc, half, sl: lambda v: v.tensor_mul(
                        h_sb[:, dc * TPC + half * 512: dc * TPC + half * 512 + 512],
                        xo_b[half][:, :], rsrep[:, sl]))(dc, half, sl),
                        waits=[("dve", sub_v)], inc="dve")
                prev_h[dc] = h_v
                prev_tail[dc] = h_v
                if stats:
                    h0v = em("vector", (lambda dc: lambda v: v.tensor_scalar(
                        h0_sb[:, dc:dc + 1], xo_b[0][:, 0:1], rsrep[:, 0:1],
                        None, ALU.mult))(dc),
                        waits=[("dve", h_v)], inc="dve")
                    prev_tail[dc] = h0v
                    aw = [("dve", h_v)]
                    if dc >= 2:
                        aw.append(("pe", prev_abs[dc - 2]))
                    ab = em("scalar", (lambda dc, slot: lambda sc: sc.activation(
                        sqbf[slot][:, :], h_sb[:, dc * TPC:(dc + 1) * TPC],
                        AF.Abs))(dc, slot), waits=aw, inc="act")
                    pv = None
                    for half in range(2):
                        mw2 = [("act", ab)]
                        if dc == 0:
                            for b in (4, 5):
                                if bfree_ffn.get(b):
                                    mw2.append(bfree_ffn[b])
                        pv = em("tensor", (lambda slot, half, dc: lambda t: t.matmul(
                            PS[4 + half][:, :], ones_bf[:, :],
                            sqbf[slot][:, half * 512:half * 512 + 512],
                            start=(dc == 0), stop=(dc == DC - 1)))(slot, half, dc),
                            waits=mw2 if half == 0 else (),
                            inc="pe" if half == 1 else None)
                    prev_abs[dc] = pv
            H_DONE = prev_h[DC - 1]
            if not stats:
                return H_DONE, None
            ABS_PE = prev_abs[DC - 1]
            srow = [rT_sb[:, 0:512], rT_sb[:, 512:1024]]
            sr_v = None
            for half in range(2):
                sr_v = em("vector", (lambda half: lambda v: v.tensor_mul(
                    srow[half][:, :], PS[4 + half][0:1, :],
                    vm_sb[:, half * 512: half * 512 + 512]))(half),
                    waits=[("pe", ABS_PE), ("aux", AUX_T)], inc="dve")
            q1 = em("vector", lambda v: v.reduce_sum(vt1[:, :], srow[0][:, :], AXX),
                    waits=[("dve", sr_v)], inc="dve")
            q2 = em("vector", lambda v: v.reduce_sum(vt2[:, :], srow[1][:, :], AXX),
                    inc="dve")
            q3 = em("vector", lambda v: v.tensor_add(stat2[:, 0:1], vt1[:, :], vt2[:, :]),
                    waits=[("dve", q2), ("dve", q1)], inc="dve")
            q4 = em("vector", lambda v: v.reduce_sum(vt1[:, :], PS[4][0:1, :], AXX),
                    waits=[("dve", q3)], inc="dve")
            q5 = em("vector", lambda v: v.reduce_sum(vt2[:, :], PS[5][0:1, :], AXX),
                    inc="dve")
            q6 = em("vector", lambda v: v.tensor_add(strg[:, 0:1], vt1[:, :], vt2[:, :]),
                    waits=[("dve", q5), ("dve", q4)], inc="dve")
            q7 = em("vector", lambda v: v.tensor_sub(stat2[:, 1:2], strg[:, 0:1],
                                                     stat2[:, 0:1]),
                    waits=[("dve", q6)], inc="dve")
            bfree_ffn[4] = ("dve", q5)
            bfree_ffn[5] = ("dve", q5)
            return H_DONE, q7

        def emit_gate(stat_dve, tag):
            S_.scope = f"gate_{tag}"
            em("sync", lambda s: s.dma_start(out=pay_in[0:1, 0:2], in_=stat2[:, :]),
               waits=[("dve", stat_dve)], inc=tag, amt=16)
            em("sync", lambda s: s.dma_start(
                out=bass.AP(pay_in.tensor, 2, [[1, 1], [32, 128], [1, 32]]),
                in_=h0_sb[:, :]), waits=[("dve", stat_dve)], inc=tag, amt=16)
            em("sync", lambda s: s.dma_start(
                out=pay_in[0:1, 4098:4104], in_=sml[:, 8:14]),
                waits=[("aux", AUX_T)], inc=tag, amt=16)
            cc = em("gpsimd", lambda g: g.collective_compute(
                "AllGather", ALU.bypass, replica_groups=GROUPS,
                ins=[pay_in[:, :]], outs=[pay_out[:, :]]),
                waits=[(tag, S_.cnt[tag])], inc="cc")
            em("sync", lambda s: s.dma_start(out=p0s[:, :], in_=pay_out[0:1, 0:2]),
               waits=[("cc", cc)], inc=tag, amt=16)
            em("sync", lambda s: s.dma_start(out=p1s[:, :], in_=pay_out[1:2, 0:2]),
               inc=tag, amt=16)
            em("sync", lambda s: s.dma_start(
                out=h0g_sb[:, :],
                in_=bass.AP(pay_out.tensor, 2, [[1, 1], [32, 128], [1, 32]])),
                inc=tag, amt=16)
            PAY_T = S_.cnt[tag]
            d1 = em("vector", lambda v: v.tensor_add(tots[:, :], p0s[:, :], p1s[:, :]),
                    waits=[(tag, PAY_T)], inc="dve")
            d2 = em("vector", lambda v: v.tensor_mul(strg[:, :], tots[:, :],
                                                     sml[:, 6:8]),
                    waits=[("dve", d1), ("aux", AUX_T)], inc="dve")
            d3 = em("vector", lambda v: v.tensor_scalar(vt1[:, :], strg[:, 0:1],
                                                        1e-8, None, ALU.add),
                    waits=[("dve", d2)], inc="dve")
            d4 = em("vector", lambda v: v.reciprocal(vt2[:, :], vt1[:, :]),
                    waits=[("dve", d3)], inc="dve")
            d5 = em("vector", lambda v: v.tensor_mul(mr1[:, :], strg[:, 1:2],
                                                     vt2[:, :]),
                    waits=[("dve", d4)], inc="dve")
            gv = None
            for kc in range(DC):
                mw = [(tag, PAY_T), ("aux", AUX_T)]
                if kc == 0 and bfree_ffn.get(6):
                    mw.append(bfree_ffn[6])
                gv = em("tensor", (lambda kc: lambda t: t.matmul(
                    PS[6][0:1, 0:NE], h0g_sb[:, kc:kc + 1],
                    gw_f[:, kc * NE:(kc + 1) * NE],
                    start=(kc == 0), stop=(kc == DC - 1)))(kc),
                    waits=mw if kc == 0 else (),
                    inc="pe" if kc == DC - 1 else None)
            e1 = em("vector", lambda v: v.tensor_scalar(row3[0][:, :], sml[:, 0:3],
                                                        mr1[:, 0:1], None, ALU.mult),
                    waits=[("dve", d5)], inc="dve")
            e2 = em("vector", lambda v: v.tensor_add(row3[1][:, :], row3[0][:, :],
                                                     sml[:, 3:6]),
                    waits=[("dve", e1)], inc="dve")
            e3 = em("vector", lambda v: v.tensor_add(row3[2][:, :], row3[1][:, :],
                                                     PS[6][0:1, 0:NE]),
                    waits=[("dve", e2), ("pe", gv)], inc="dve")
            gb = em("scalar", lambda sc: sc.copy(gbrow_bf[:, :], row3[1][:, :]),
                    waits=[("dve", e2)], inc="act")
            f1 = em("vector", lambda v: v.reduce_max(g0m[:, :], row3[2][:, :], AXX),
                    waits=[("dve", e3)], inc="dve")
            f2 = em("vector", lambda v: v.tensor_reduce(g0n[:, :], row3[2][:, :],
                                                        AXX, ALU.min),
                    waits=[("dve", e3)], inc="dve")
            f3 = em("vector", lambda v: v.tensor_scalar(row3[3][:, :], row3[2][:, :],
                                                        g0m[:, 0:1], None, ALU.is_ge),
                    waits=[("dve", f1)], inc="dve")
            f4 = em("vector", lambda v: v.tensor_scalar(row3[4][:, :], row3[2][:, :],
                                                        g0n[:, 0:1], None, ALU.is_le),
                    waits=[("dve", f2)], inc="dve")
            f5 = em("vector", lambda v: v.tensor_add(indt[:, :], row3[3][:, :],
                                                     row3[4][:, :]),
                    waits=[("dve", f4), ("dve", f3)], inc="dve")
            f6 = em("vector", lambda v: v.tensor_scalar(row3[5][:, :], indt[:, :],
                                                        -1.0, None, ALU.mult),
                    waits=[("dve", f5)], inc="dve")
            f7 = em("vector", lambda v: v.tensor_scalar(indt[:, :], row3[5][:, :],
                                                        1.0, None, ALU.add),
                    waits=[("dve", f6)], inc="dve")
            i1 = em("tensor", lambda t: t.matmul(PS[6][:, 0:NE], onesf_sb[:, :],
                                                 row3[3][:, :], start=True, stop=True),
                    waits=[("dve", f3), ("dve", e3)], inc="pe")
            i1c = em("scalar", lambda sc: sc.copy(ind_rep[0][:, :], PS[6][:, 0:NE]),
                     waits=[("pe", i1)], inc="act")
            i2 = em("tensor", lambda t: t.matmul(PS[6][:, 0:NE], onesf_sb[:, :],
                                                 indt[:, :], start=True, stop=True),
                    waits=[("dve", f7), ("act", i1c)], inc="pe")
            i2c = em("scalar", lambda sc: sc.copy(ind_rep[1][:, :], PS[6][:, 0:NE]),
                     waits=[("pe", i2)], inc="act")

            prev_ge = None
            for tt in range(8):
                mw = [("auxg", AUXG_T)]
                if prev_ge is not None:
                    mw.append(("act", prev_ge))
                elif bfree_ffn.get(7):
                    mw.append(bfree_ffn[7])
                for kc in range(DC):
                    em("tensor", (lambda tt, kc: lambda t: t.matmul(
                        PS[7][:, 0:NE],
                        h_sb[:, kc * TPC + tt * 128: kc * TPC + tt * 128 + 128],
                        gw_bf[:, kc * NE:(kc + 1) * NE],
                        start=(kc == 0), stop=False))(tt, kc),
                        waits=mw if kc == 0 else ())
                gpe = em("tensor", lambda t: t.matmul(
                    PS[7][:, 0:NE], ones_bf[0:1, :], gbrow_bf[:, :],
                    start=False, stop=True),
                    waits=[("act", gb)], inc="pe")
                ge = em("scalar", lambda sc: sc.copy(g_t[:, :], PS[7][:, 0:NE]),
                        waits=[("pe", gpe), ("dve", S_.cnt["dve"])], inc="act")
                prev_ge = ge
                s1 = em("vector", lambda v: v.reduce_max(nm1[:, :], g_t[:, :], AXX,
                                                         negate=True),
                        waits=[("act", ge)], inc="dve")
                s2 = em("scalar", lambda sc: sc.activation(e_t[:, :], g_t[:, :],
                                                           AF.Exp, bias=nm1[:, 0:1],
                                                           scale=np.float32(1.0),
                                                           accum_out=ssum[:, :]),
                        waits=[("dve", s1)], inc="act")
                s3 = em("vector", lambda v: v.reciprocal(rinv[:, :], ssum[:, :]),
                        waits=[("act", s2)], inc="dve")
                s4 = em("vector", lambda v: v.tensor_scalar(sc_t[:, :], e_t[:, :],
                                                            rinv[:, 0:1], None,
                                                            ALU.mult),
                        waits=[("dve", s3)], inc="dve")
                u1 = em("vector", lambda v: v.reduce_max(m1c[:, :], sc_t[:, :], AXX),
                        waits=[("dve", s4)], inc="dve")
                u2 = em("vector", lambda v: v.tensor_reduce(m3c[:, :], sc_t[:, :],
                                                            AXX, ALU.min),
                        waits=[("dve", s4)], inc="dve")
                u3 = em("vector", lambda v: v.reduce_sum(s2c[:, :], sc_t[:, :], AXX),
                        waits=[("dve", s4)], inc="dve")
                u4 = em("vector", lambda v: v.tensor_sub(denc[:, :], s2c[:, :],
                                                         m3c[:, :]),
                        waits=[("dve", u3), ("dve", u2)], inc="dve")
                u5 = em("vector", lambda v: v.tensor_sub(m2c[:, :], denc[:, :],
                                                         m1c[:, :]),
                        waits=[("dve", u4), ("dve", u1)], inc="dve")
                u6 = em("vector", lambda v: v.tensor_scalar(den2c[:, :], denc[:, :],
                                                            1e-8, None, ALU.add),
                        waits=[("dve", u4)], inc="dve")
                u7 = em("vector", lambda v: v.reciprocal(dinvc[:, :], den2c[:, :]),
                        waits=[("dve", u6)], inc="dve")
                u8 = em("vector", lambda v: v.tensor_scalar(dinv5[:, :], dinvc[:, :],
                                                            0.5, None, ALU.mult),
                        waits=[("dve", u7)], inc="dve")
                u9 = em("vector", (lambda tt: lambda v: v.tensor_mul(
                    tv0[:, tt:tt + 1], m1c[:, :], dinv5[:, :]))(tt),
                    waits=[("dve", u8)], inc="dve")
                u10 = em("vector", (lambda tt: lambda v: v.tensor_mul(
                    tv1[:, tt:tt + 1], m2c[:, :], dinv5[:, :]))(tt),
                    waits=[("dve", u8), ("dve", u5)], inc="dve")
                c1 = em("vector", (lambda tt: lambda v: v.tensor_scalar(
                    c_t[0][:, :], ind_rep[0][:, :], tv0[:, tt:tt + 1], None,
                    ALU.mult))(tt),
                    waits=[("dve", u9), ("act", i1c)], inc="dve")
                c2 = em("vector", (lambda tt: lambda v: v.tensor_scalar(
                    c_t[1][:, :], ind_rep[1][:, :], tv1[:, tt:tt + 1], None,
                    ALU.mult))(tt),
                    waits=[("dve", u10), ("act", i2c)], inc="dve")
                c3 = em("vector", lambda v: v.tensor_add(c_t[2][:, :], c_t[0][:, :],
                                                         c_t[1][:, :]),
                        waits=[("dve", c2), ("dve", c1)], inc="dve")
                for e in range(NE):
                    tm = em("tensor", (lambda e: lambda t: t.matmul(
                        PS[6][0:1, 0:128], c_t[2][:, e:e + 1], ident_sb[:, :],
                        start=True, stop=True))(e),
                        waits=[("dve", c3), ("act", S_.cnt["act"])], inc="pe")
                    em("scalar", (lambda tt, e: lambda sc: sc.copy(
                        crow[e][:, tt * 128:(tt + 1) * 128],
                        PS[6][0:1, 0:128]))(tt, e),
                        waits=[("pe", tm)], inc="act")
            crep_last = None
            for e in range(NE):
                for half in range(2):
                    bank = 6 + half
                    rm = em("tensor", (lambda e, half, bank: lambda t: t.matmul(
                        PS[bank][:, :], ones_bf[0:1, :],
                        crow[e][:, half * 512: half * 512 + 512],
                        start=True, stop=True))(e, half, bank),
                        waits=[("act", S_.cnt["act"])], inc="pe")
                    crep_last = em("scalar", (lambda e, half, bank: lambda sc: sc.copy(
                        crep[e][:, half * 512: half * 512 + 512],
                        PS[bank][:, :]))(e, half, bank),
                        waits=[("pe", rm)], inc="act")
            bfree_ffn[6] = ("act", crep_last)
            bfree_ffn[7] = ("act", crep_last)
            return crep_last

        def emit_ffn_call(gate_fn, h_dve, x_src, x_dst, final_dst):
            crep_act = None
            for e in range(NE):
                S_.scope = f"ffn{'1' if x_dst is not final_dst or final_dst is xdram else '?'}_e{e}A" if False else f"ffn_e{e}A_{id(x_src) % 97}"
                wl = {0: emit_wload(0, T["w1"][e, 0]), 1: emit_wload(1, T["w1"][e, 1])}
                act_done = {}
                for fc in range(DC):
                    wsem, wval = wl[fc]
                    for half in range(2):
                        bank = 2 * (fc % 2) + half
                        mmw = [(wsem, wval)]
                        if bfree_ffn.get(bank):
                            mmw.append(bfree_ffn[bank])
                        if fc == 0 and half == 0:
                            mmw += [("dve", h_dve), ("aux", AUX_T)]
                        pv = None
                        for kc in range(DC):
                            pv = em("tensor", (lambda fc, half, kc, bank: lambda t:
                                    t.matmul(PS[bank][:, :],
                                             wbuf[fc % 2][:, kc * 128:(kc + 1) * 128],
                                             h_sb[:, kc * TPC + half * 512:
                                                  kc * TPC + half * 512 + 512],
                                             start=(kc == 0), stop=(kc == DC - 1))
                                    )(fc, half, kc, bank),
                                    waits=mmw if kc == 0 else (),
                                    inc="pe" if kc == DC - 1 else None)
                        av = em("scalar", (lambda fc, half, e, bank: lambda sc:
                                sc.activation(a_sb[:, fc * TPC + half * 512:
                                                   fc * TPC + half * 512 + 512],
                                              PS[bank][:, :], ACTF[e],
                                              bias=b1_sb[:, e * DC + fc:e * DC + fc + 1],
                                              scale=np.float32(1.0)))(fc, half, e, bank),
                                waits=[("pe", pv)], inc="act")
                        bfree_ffn[bank] = ("act", av)
                        act_done[(fc, half)] = av
                    wslot_free[fc % 2] = S_.cnt["pe"]
                    if fc + 2 < DC:
                        wl[fc + 2] = emit_wload(fc % 2, T["w1"][e, fc + 2])
                ACT_H = [act_done[(DC - 1, 0)], act_done[(DC - 1, 1)]]
                if e == 0 and gate_fn is not None:
                    crep_act = gate_fn()
                elif crep_act is None:
                    crep_act = gate_fn
                S_.scope = f"ffn_e{e}B_{id(x_src) % 97}"
                wl = {0: emit_wload(0, T["w2"][e, 0]), 1: emit_wload(1, T["w2"][e, 1])}
                b2l = em("gpsimd", (lambda e: lambda g: g.dma_start(
                    out=b2row[:, :], in_=T["b2"][e:e + 1, 0:1024]))(e),
                    waits=[("pe", wslot_free[0])], inc="b2l", amt=16)
                rmw = {}
                dst = final_dst if e == NE - 1 else x_dst
                rsrc = x_src if e == 0 else x_dst
                lxv = {}

                def emit_xload(idx):
                    dcq2, half2 = idx // 2, idx % 2
                    slot = idx % 3
                    xw = [store_of_chunk.get((dcq2, half2)),
                          ("dve", xi_prev_add[slot])]
                    lxv[idx] = (slot, em("sync",
                        (lambda dcq2, half2, slot, rsrc: lambda s: s.dma_start(
                            out=xi_b[slot][:, :],
                            in_=rsrc[dcq2, :, half2 * 512: half2 * 512 + 512])
                         )(dcq2, half2, slot, rsrc),
                        waits=xw, inc=f"rxl{slot}", amt=16))

                emit_xload(0)
                emit_xload(1)
                emit_xload(2)
                for dcq in range(DC):
                    if dcq in (8, 16, 24):
                        b2l = em("gpsimd", (lambda e, dcq: lambda g: g.dma_start(
                            out=b2row[:, :],
                            in_=T["b2"][e:e + 1, dcq * 128:(dcq + 8) * 128]))(e, dcq),
                            waits=[("pe", S_.cnt["pe"])], inc="b2l", amt=16)
                    wsem, wval = wl[dcq]
                    for half in range(2):
                        bank = 4 + 2 * (dcq % 2) + half
                        idx = dcq * 2 + half
                        slot, lx = lxv[idx]
                        mmw = [(wsem, wval), ("act", ACT_H[half]),
                               ("b2l", b2l), ("auxg", AUXG_T)]
                        if bfree_ffn.get(bank):
                            mmw.append(bfree_ffn[bank])
                        em("tensor", (lambda e, dcq, half, bank: lambda t: t.matmul(
                            PS[bank][:, :],
                            b2row[:, (dcq % 8) * 128:(dcq % 8) * 128 + 128],
                            onesrow_bf[0:1, :],
                            start=True, stop=False))(e, dcq, half, bank),
                            waits=mmw)
                        pv = None
                        for fc in range(DC):
                            pv = em("tensor", (lambda fc, dcq, half, bank: lambda t:
                                    t.matmul(PS[bank][:, :],
                                             wbuf[dcq % 2][:, fc * 128:(fc + 1) * 128],
                                             a_sb[:, fc * TPC + half * 512:
                                                  fc * TPC + half * 512 + 512],
                                             start=False, stop=(fc == DC - 1))
                                    )(fc, dcq, half, bank),
                                    inc="pe" if fc == DC - 1 else None)
                        mw2 = [("pe", pv), ("act", crep_act),
                               ("dve", tmp_prev[half])]
                        mu = em("vector", (lambda half, bank, e: lambda v:
                                v.tensor_mul(tmp_b[half][:, :], PS[bank][:, :],
                                             crep[e][:, half * 512:half * 512 + 512])
                                )(half, bank, e), waits=mw2, inc="dve")
                        bfree_ffn[bank] = ("dve", mu)
                        oslot = idx % 2
                        vw = [("dve", mu), (f"rxl{slot}", lx),
                              xo_prev_store[oslot]]
                        dv = em("vector", (lambda slot, half, oslot: lambda v:
                                v.tensor_add(xo_b[oslot][:, :], xi_b[slot][:, :],
                                             tmp_b[half][:, :])
                                )(slot, half, oslot), waits=vw, inc="dve")
                        tmp_prev[half] = dv
                        xi_prev_add[slot] = dv
                        sv = em("sync", (lambda dcq, half, oslot, dst: lambda s:
                                s.dma_start(out=dst[dcq, :, half * 512:half * 512 + 512],
                                            in_=xo_b[oslot][:, :]))(dcq, half, oslot, dst),
                                waits=[("dve", dv)], inc=f"rxs{oslot}", amt=16)
                        rmw[idx] = (dv, (f"rxs{oslot}", sv))
                        xo_prev_store[oslot] = (f"rxs{oslot}", sv)
                        store_of_chunk[(dcq, half)] = (f"rxs{oslot}", sv)
                        if idx + 3 < 2 * DC:
                            emit_xload(idx + 3)
                    wslot_free[dcq % 2] = S_.cnt["pe"]
                    if dcq + 2 < DC:
                        wl[dcq + 2] = emit_wload(dcq % 2, T["w2"][e, dcq + 2])

        def emit_attention(h_dve):
            S_.scope = "attn_kv"
            S_.barrier()
            wk_l = emit_wload(0, T["wk"])
            wv_l = emit_wload(1, T["wv"])
            kev = None
            for half in range(2):
                pv = None
                for kc in range(DC):
                    mw = [wk_l, ("dve", h_dve)]
                    if kc == 0 and bfree_ffn.get(half):
                        mw.append(bfree_ffn[half])
                    pv = em("tensor", (lambda half, kc: lambda t: t.matmul(
                        PS[half][:, :], wbuf[0][:, kc * 128:(kc + 1) * 128],
                        h_sb[:, kc * TPC + half * 512: kc * TPC + half * 512 + 512],
                        start=(kc == 0), stop=(kc == DC - 1)))(half, kc),
                        waits=mw if kc == 0 else (),
                        inc="pe" if kc == DC - 1 else None)
                kev = em("scalar", (lambda half: lambda sc: sc.copy(
                    klocal[:, half * 512: half * 512 + 512], PS[half][:, :]))(half),
                    waits=[("pe", pv)], inc="act")
                bfree_ffn[half] = ("act", kev)
            em("sync", lambda s: s.dma_start(out=kag_in[:, :], in_=klocal[:, :]),
               waits=[("act", kev)], inc="kst", amt=16)
            prev_ve = None
            for g in range(8):
                pv = None
                for kc in range(DC):
                    mw = []
                    if kc == 0:
                        mw = [wv_l, ("dve", h_dve)]
                        if prev_ve is not None:
                            mw.append(("act", prev_ve))
                        elif bfree_ffn.get(2):
                            mw.append(bfree_ffn[2])
                    pv = em("tensor", (lambda g, kc: lambda t: t.matmul(
                        PS[2][:, 0:128],
                        h_sb[:, kc * TPC + g * 128: kc * TPC + g * 128 + 128],
                        wbuf[1][:, kc * 128:(kc + 1) * 128],
                        start=(kc == 0), stop=(kc == DC - 1)))(g, kc),
                        waits=mw, inc="pe" if kc == DC - 1 else None)
                sl = g % 2
                vw = [("pe", pv)]
                if g >= 2:
                    vw.append((f"vst{sl}", 16 * (g // 2)))
                ve = em("scalar", (lambda sl: lambda sc: sc.copy(
                    vloc[sl][:, :], PS[2][:, 0:128]))(sl),
                    waits=vw, inc="act")
                prev_ve = ve
                em("sync", (lambda g, sl: lambda s: s.dma_start(
                    out=vag_in[g * 128:(g + 1) * 128, :], in_=vloc[sl][:, :]))(g, sl),
                    waits=[("act", ve)], inc=f"vst{sl}", amt=16)
            bfree_ffn[2] = ("act", prev_ve)
            KV_ST = [("kst", S_.cnt["kst"]), ("vst0", S_.cnt["vst0"]),
                     ("vst1", S_.cnt["vst1"])]
            em("gpsimd", lambda g: g.collective_compute(
                "AllGather", ALU.bypass, replica_groups=GROUPS,
                ins=[kag_in[:, :]], outs=[kag_out[:, :]]),
                waits=KV_ST, inc="cc")
            cc2 = em("gpsimd", lambda g: g.collective_compute(
                "AllGather", ALU.bypass, replica_groups=GROUPS,
                ins=[vag_in[:, :]], outs=[vag_out[:, :]]),
                inc="cc")

            S_.scope = "attn_q"
            wl = {0: emit_wload(0, T["wq"][0]), 1: emit_wload(1, T["wq"][1])}
            qev = {}
            for hh in range(NHEAD):
                wsem, wval = wl[hh]
                for half in range(2):
                    bank = 4 + 2 * (hh % 2) + half
                    mmw = [(wsem, wval)]
                    if hh >= 2:
                        mmw.append(("act", qev[(hh - 2, half)]))
                    elif bfree_ffn.get(bank):
                        mmw.append(bfree_ffn[bank])
                    pv = None
                    for kc in range(DC):
                        pv = em("tensor", (lambda hh, half, kc, bank: lambda t:
                                t.matmul(PS[bank][:, :],
                                         wbuf[hh % 2][:, kc * 128:(kc + 1) * 128],
                                         h_sb[:, kc * TPC + half * 512:
                                              kc * TPC + half * 512 + 512],
                                         start=(kc == 0), stop=(kc == DC - 1))
                                )(hh, half, kc, bank),
                                waits=mmw if kc == 0 else (),
                                inc="pe" if kc == DC - 1 else None)
                    qw = [("pe", pv)]
                    if hh < 2:
                        qw += KV_ST
                    qv = em("scalar", (lambda hh, half, bank: lambda sc: sc.activation(
                        a_sb[:, hh * TPC + half * 512: hh * TPC + half * 512 + 512],
                        PS[bank][:, :], AF.Copy, bias=0.0,
                        scale=np.float32(QSCALE)))(hh, half, bank),
                        waits=qw, inc="act")
                    qev[(hh, half)] = qv
                wslot_free[hh % 2] = S_.cnt["pe"]
                if hh + 2 < NHEAD:
                    wl[hh + 2] = emit_wload(hh % 2, T["wq"][hh + 2])
            Q_PE = S_.cnt["pe"]
            abank_free = {4: ("act", qev[(NHEAD - 2, 0)]),
                          5: ("act", qev[(NHEAD - 2, 1)]),
                          6: ("act", qev[(NHEAD - 1, 0)]),
                          7: ("act", qev[(NHEAD - 1, 1)])}

            for g in range(16):
                rk, li = g % 2, g // 2
                mw = [("cc", cc2), ("pe", Q_PE)] if g == 0 else ()
                em("sync", (lambda g, rk, li: lambda s: s.dma_start(
                    out=K_sb[:, g * 128:(g + 1) * 128],
                    in_=kag_out[rk * 128:(rk + 1) * 128,
                                li * 128:(li + 1) * 128]))(g, rk, li),
                    waits=mw, inc="kv", amt=16)
                em("sync", (lambda g, rk, li: lambda s: s.dma_start(
                    out=V_sb[:, g * 128:(g + 1) * 128],
                    in_=vag_out[rk * TPC + li * 128: rk * TPC + li * 128 + 128, :]
                    ))(g, rk, li), inc="kv", amt=16)
            KV_RB = S_.cnt["kv"]

            pt_sl4 = [h_sb[:, i * 512:(i + 1) * 512] for i in range(4)]
            ptm_sl2 = [h_sb[:, 6144 + i * 128: 6272 + i * 128] for i in range(4)]
            aot_sl2 = [h_sb[:, 6656 + i * 128: 6784 + i * 128] for i in range(2)]
            S_.scope = "attn_heads"
            bfree = {i: bfree_ffn.get(i) for i in range(4)}
            bfree.update(abank_free)
            abank_free2 = {b2_: [bfree[b2_]] for b2_ in range(4, 8)}
            pt_read = {i: None for i in range(4)}
            ptm_read = {i: None for i in range(4)}
            aot_read = {0: 0, 1: 0}
            rrep_read = None
            rT_write_ok = [None] * 8
            SKEW = 3
            for hh in range(NHEAD):
                for qg in range(2):
                    nkt = 8 if qg == 0 else 16
                    exp_v = {}
                    mask_v = {}
                    zred_last = {}
                    for i in range(nkt + SKEW):
                        if i < nkt:
                            kt = i
                            sb_ = kt % 3
                            slot = kt % 4
                            stw = [("act", qev[(hh, qg)]), bfree[sb_], ("kv", KV_RB)]
                            spv = em("tensor", (lambda hh, qg, kt, sb_: lambda t:
                                    t.matmul(PS[sb_][:, :],
                                             K_sb[:, kt * 128:(kt + 1) * 128],
                                             a_sb[:, hh * TPC + qg * 512:
                                                  hh * TPC + qg * 512 + 512],
                                             start=True, stop=True))(hh, qg, kt, sb_),
                                    waits=stw, inc="pe")
                            pe_v = em("scalar", (lambda sb_, slot: lambda sc:
                                    sc.activation(pt_sl4[slot][:, :], PS[sb_][:, :],
                                                  AF.Exp, bias=np.float32(-CSHIFT),
                                                  scale=np.float32(1.0)))(sb_, slot),
                                    waits=[pt_read[slot], ("pe", spv)], inc="act")
                            exp_v[kt] = pe_v
                            bfree[sb_] = ("act", pe_v)
                            if 8 * qg <= kt < 8 * qg + 8:
                                jj = kt // 2 - 4 * qg
                                msl = kt % 4
                                mask_v[kt] = em("vector",
                                    (lambda slot, kt, jj, msl: lambda v:
                                     v.tensor_mul(ptm_sl2[msl][:, :],
                                                  pt_sl4[slot][:, jj * 128: jj * 128 + 128],
                                                  ptmask_sb[kt % 2][:, :]))(slot, kt, jj, msl),
                                    waits=[("act", pe_v), ("auxg", AUXG_T),
                                           ptm_read[msl]], inc="dve")
                        j2 = i - SKEW
                        if 0 <= j2 < nkt:
                            kt = j2
                            slot = kt % 4
                            use_mask = (8 * qg <= kt < 8 * qg + 8)
                            lastpv = None
                            for jj2 in range(4):
                                jloc = 4 * qg + jj2
                                if 2 * jloc + 1 < kt:
                                    continue
                                bb = 4 + jj2
                                first = (kt == 0)
                                stop_z = (kt == 2 * jloc + 1)
                                if use_mask and jj2 == kt // 2 - 4 * qg:
                                    rhs_ap = (lambda m: lambda: ptm_sl2[m][:, :])(kt % 4)
                                    pw = [("dve", mask_v[kt])]
                                else:
                                    rhs_ap = (lambda s, q: lambda:
                                              pt_sl4[s][:, q * 128:q * 128 + 128])(slot, jj2)
                                    pw = [("act", exp_v[kt])]
                                if first:
                                    pw += abank_free2[bb]
                                if stop_z:
                                    em("tensor", (lambda rhs_ap, bb, kt:
                                            lambda t: t.matmul(
                                                PS[bb][:, 128:256],
                                                ones_bf[:, :],
                                                rhs_ap(), start=False, stop=False))(
                                                rhs_ap, bb, kt),
                                            waits=pw, inc="pe")
                                    lastpv = em("tensor", (lambda rhs_ap, bb, kt:
                                            lambda t: t.matmul(
                                                PS[bb][:, 0:128],
                                                V_sb[:, kt * 128:(kt + 1) * 128],
                                                rhs_ap(), start=False, stop=True))(
                                                rhs_ap, bb, kt),
                                            inc="pe")
                                    zred_last[jj2] = lastpv
                                else:
                                    em("tensor", (lambda rhs_ap, bb, first, kt:
                                            lambda t: t.matmul(
                                                PS[bb][:, 0:128],
                                                V_sb[:, kt * 128:(kt + 1) * 128],
                                                rhs_ap(), start=first, stop=False))(
                                                rhs_ap, bb, first, kt),
                                            waits=pw, inc="pe")
                                    lastpv = em("tensor", (lambda rhs_ap, bb, kt:
                                            lambda t: t.matmul(
                                                PS[bb][:, 128:256],
                                                ones_bf[:, :],
                                                rhs_ap(), start=False, stop=False))(
                                                rhs_ap, bb, kt),
                                            inc="pe")
                                pt_read[slot] = ("pe", lastpv)
                                if use_mask and jj2 == kt // 2 - 4 * qg:
                                    ptm_read[kt % 4] = ("pe", lastpv)
                    for jj2 in range(4):
                        jloc = 4 * qg + jj2
                        bb = 4 + jj2
                        sl = jj2 % 2
                        rv = em("vector", (lambda bb, jloc: lambda v: v.reciprocal(
                            rT_sb[:, jloc * 128:(jloc + 1) * 128],
                            PS[bb][0:1, 128:256]))(bb, jloc),
                            waits=[("pe", zred_last[jj2]),
                                   ("pe", rT_write_ok[jloc])], inc="dve")
                        aev = em("scalar", (lambda bb, sl: lambda sc: sc.copy(
                            aot_sl2[sl][:, :], PS[bb][:, 0:128]))(bb, sl),
                            waits=[("pe", zred_last[jj2]), ("dve", aot_read[sl])],
                            inc="act")
                        abank_free2[bb] = [("act", aev), ("dve", rv)]
                        rmw_ = [("dve", rv), ("aux", AUX_T), ("dve", rrep_read)]
                        rm = em("tensor", (lambda jloc: lambda t: t.matmul(
                            PS[3][:, 0:128], onesf_sb[:, :],
                            rT_sb[:, jloc * 128:(jloc + 1) * 128],
                            start=True, stop=True))(jloc),
                            waits=rmw_, inc="pe")
                        rT_write_ok[jloc] = rm
                        sv = em("vector", (lambda hh, jloc, sl: lambda v:
                                v.tensor_mul(a_sb[:, hh * TPC + jloc * 128:
                                                  hh * TPC + jloc * 128 + 128],
                                             aot_sl2[sl][:, :], PS[3][:, 0:128]))(
                                             hh, jloc, sl),
                                waits=[("pe", rm), ("act", aev)], inc="dve")
                        aot_read[sl] = sv
                        rrep_read = sv
            AODONE = S_.cnt["dve"]
            for b in range(4, 8):
                bfree_ffn[b] = bfree[b]

            S_.scope = "attn_wo"
            S_.barrier()
            wl = {0: emit_wload(0, T["wo"][0]), 1: emit_wload(1, T["wo"][1])}
            rmw = {}
            lxv = {}

            def emit_xload2(idx):
                dcq2, half2 = idx // 2, idx % 2
                slot = idx % 3
                xw = [store_of_chunk.get((dcq2, half2)),
                      ("dve", xi_prev_add[slot])]
                lxv[idx] = (slot, em("sync",
                    (lambda dcq2, half2, slot: lambda s: s.dma_start(
                        out=xi_b[slot][:, :],
                        in_=xdram[dcq2, :, half2 * 512: half2 * 512 + 512])
                     )(dcq2, half2, slot),
                    waits=xw, inc=f"rxl{slot}", amt=16))

            emit_xload2(0)
            emit_xload2(1)
            emit_xload2(2)
            for dcq in range(DC):
                wsem, wval = wl[dcq]
                for half in range(2):
                    bank = (dcq % 2) * 2 + half
                    idx = dcq * 2 + half
                    slot, lx = lxv[idx]
                    mmw = [(wsem, wval), ("dve", AODONE)]
                    if idx >= 2:
                        mmw.append(("dve", rmw[idx - 2][0]))
                    pv = None
                    for hh in range(NHEAD):
                        pv = em("tensor", (lambda dcq, half, hh, bank: lambda t:
                                t.matmul(PS[bank][:, :],
                                         wbuf[dcq % 2][:, hh * 128:(hh + 1) * 128],
                                         a_sb[:, hh * TPC + half * 512:
                                              hh * TPC + half * 512 + 512],
                                         start=(hh == 0), stop=(hh == NHEAD - 1))
                                )(dcq, half, hh, bank),
                                waits=mmw if hh == 0 else (),
                                inc="pe" if hh == NHEAD - 1 else None)
                    oslot = idx % 2
                    vw = [("pe", pv), (f"rxl{slot}", lx),
                          xo_prev_store[oslot]]
                    dv = em("vector", (lambda slot, bank, oslot: lambda v:
                            v.tensor_add(xo_b[oslot][:, :], xi_b[slot][:, :],
                                         PS[bank][:, :])
                            )(slot, bank, oslot), waits=vw, inc="dve")
                    bfree_ffn[bank] = ("dve", dv)
                    xi_prev_add[slot] = dv
                    sv = em("sync", (lambda dcq, half, oslot: lambda s: s.dma_start(
                            out=xdram[dcq, :, half * 512: half * 512 + 512],
                            in_=xo_b[oslot][:, :]))(dcq, half, oslot),
                            waits=[("dve", dv)], inc=f"rxs{oslot}", amt=16)
                    rmw[idx] = (dv, (f"rxs{oslot}", sv))
                    xo_prev_store[oslot] = (f"rxs{oslot}", sv)
                    store_of_chunk[(dcq, half)] = (f"rxs{oslot}", sv)
                    if idx + 3 < 2 * DC:
                        emit_xload2(idx + 3)
                wslot_free[dcq % 2] = S_.cnt["pe"]
                if dcq + 2 < DC:
                    wl[dcq + 2] = emit_wload(dcq % 2, T["wo"][dcq + 2])

        h1, st1 = emit_ln(T["xin"], stats=True, src_is_x=False)
        emit_ffn_call(lambda: emit_gate(st1, "pg1"), h1, T["xin"], xdram,
                      out if stages == 1 else xdram)
        if stages >= 2:
            h2, _ = emit_ln(xdram, stats=False, src_is_x=True)
            emit_attention(h2)
            if stages == 2:
                for dcq in range(DC):
                    w0 = store_of_chunk.get((dcq, 0))
                    w1_ = store_of_chunk.get((dcq, 1))
                    em("sync", (lambda dcq: lambda s: s.dma_start(
                        out=out[dcq], in_=xdram[dcq]))(dcq),
                        waits=[w0, w1_], inc="ost", amt=16)
                em("sync", lambda s: s.nop(), waits=[("ost", S_.cnt["ost"])])
        if stages >= 3:
            h3, st3 = emit_ln(xdram, stats=True, src_is_x=True)
            emit_ffn_call(lambda: emit_gate(st3, "pg3"), h3, xdram, xdram, out)

        for slot in range(2):
            n = S_.cnt.get(f"rxs{slot}", 0)
            if n > 0:
                em("sync", lambda s: s.nop(), waits=[(f"rxs{slot}", n)])

        block = E(nc.Block())
        S_.install(block)

    return nc


def _stage_inputs(L0_attn_x, L0_attn_wq, L0_attn_wk, L0_attn_wv, L0_attn_wo,
                  L0_ffn_gate_w, L0_ffn_gate_b, L0_ffn_w1, L0_ffn_b1,
                  L0_ffn_w2, L0_ffn_b2):
    f32 = np.float32
    w1s = np.ascontiguousarray(
        np.asarray(L0_ffn_w1, f32).reshape(NE, DC, 128, DC, 128)
        .transpose(0, 3, 2, 1, 4).reshape(NE, DC, 128, D))
    w2s = np.ascontiguousarray(
        np.asarray(L0_ffn_w2, f32).reshape(NE, DC, 128, DC, 128)
        .transpose(0, 3, 2, 1, 4).reshape(NE, DC, 128, D))
    wqs = np.ascontiguousarray(
        np.asarray(L0_attn_wq, f32).reshape(DC, 128, NHEAD, 128)
        .transpose(2, 1, 0, 3).reshape(NHEAD, 128, D))
    wks = np.ascontiguousarray(
        np.asarray(L0_attn_wk, f32).reshape(DC, 128, HD)
        .transpose(1, 0, 2).reshape(128, D))
    wvs = np.ascontiguousarray(
        np.asarray(L0_attn_wv, f32).reshape(DC, 128, HD)
        .transpose(1, 0, 2).reshape(128, D))
    wos = np.ascontiguousarray(
        np.asarray(L0_attn_wo, f32).reshape(NHEAD, 128, DC, 128)
        .transpose(2, 1, 0, 3).reshape(DC, 128, D))
    gws = np.ascontiguousarray(
        np.asarray(L0_ffn_gate_w, f32)[0:D, :].reshape(DC, 128, NE)
        .transpose(1, 0, 2).reshape(128, DC * NE))
    b1s = np.ascontiguousarray(
        np.asarray(L0_ffn_b1, f32).reshape(NE, DC, 128)
        .transpose(2, 0, 1).reshape(128, NE * DC))
    b2s = np.ascontiguousarray(np.asarray(L0_ffn_b2, f32))
    smalls = np.zeros((1, 16), f32)
    smalls[0, 0:3] = np.asarray(L0_ffn_gate_w, f32)[D, :]
    smalls[0, 3:6] = np.asarray(L0_ffn_gate_b, f32)
    smalls[0, 6] = 1.0 / (VIS * D)
    smalls[0, 7] = 1.0 / ((S - VIS) * D)
    onesf = np.ones((1, 128), f32)
    onesrow = np.ones((1, 512), f32)
    ones2d = np.ones((128, 128), f32)
    identf = np.eye(128, dtype=f32)

    x = np.asarray(L0_attn_x, f32)
    in_maps = []
    for c in range(8):
        b, parity = c // 2, c % 2
        blocks = [2 * j + parity for j in range(8)]
        tok = np.concatenate([np.arange(g * 128, (g + 1) * 128) for g in blocks])
        xs = np.ascontiguousarray(
            x[b][tok].reshape(TPC, DC, 128).transpose(1, 2, 0))
        vm = (tok < VIS).astype(f32).reshape(1, TPC)
        zm = np.zeros((2, 128, 512), f32)
        for tsel in range(2):
            off = parity * 128 + tsel * 256
            kcol = np.arange(512)[None, :]
            rows = np.arange(128)[:, None]
            zm[tsel] = np.where(kcol < off + rows + 1, 0.0, -1e30)
        kk = np.arange(128)[:, None]
        qq = np.arange(128)[None, :]
        tri = (kk <= qq).astype(f32)
        pm = np.zeros((2, 128, 128), f32)
        pm[0] = tri if parity == 0 else np.ones((128, 128), f32)
        pm[1] = np.zeros((128, 128), f32) if parity == 0 else tri
        in_maps.append(dict(
            xin=xs, w1=w1s, w2=w2s, wq=wqs, wk=wks, wv=wvs, wo=wos,
            gw=gws, b1=b1s, b2=b2s, smalls=smalls, vmask=vm,
            zmask=zm, ptmask=pm, onesf=onesf, onesrow=onesrow,
            ones2d=ones2d, identf=identf))
    return in_maps


def kernel(**inputs):
    in_maps = _stage_inputs(**inputs)
    nc = build_program(debug=_DEBUG, stages=_STAGES)
    res = run_bass_kernel_spmd(nc, in_maps, core_ids=list(range(8)))
    outp = np.zeros((B, S, D), np.float32)
    for c in range(8):
        b, parity = c // 2, c % 2
        arr = res.results[c]["out"]
        tokmaj = arr.transpose(2, 0, 1).reshape(TPC, D)
        for j in range(8):
            g = 2 * j + parity
            outp[b, g * 128:(g + 1) * 128, :] = tokmaj[j * 128:(j + 1) * 128, :]
    return outp
